# version 12
# baseline (speedup 1.0000x reference)
import sys
import contextlib
import numpy as np

sys.path.insert(0, "/opt/trn_rl_repo")
import concourse.bass as bass
import concourse.mybir as mybir
from concourse.bass_utils import run_bass_kernel_spmd
from concourse.alu_op_type import AluOpType as ALU

F32 = mybir.dt.float32
BF16 = mybir.dt.bfloat16
AF = mybir.ActivationFunctionType

D = 1024
DC = 8
HI = 256
HC = HI // 128
NTOK = 256
NSTEP = 8
E = 8
LR = 1e-3
C0 = 2.0 / (2048.0 * 1024.0)
NEG_LRC0 = -LR * C0

C_EPS_RMS, C_INV_D, C_NEG_BIG, C_ZERO, C_EPS_LN, C_ONE, C_NEG1, C_NLRC0, C_TWO, C_NEGHALF = range(10)
NCI = 10
NGV = 11
T_RUN = NSTEP
CC_STUB = False
PHASE = 7
I_FF = 3584
NIB = I_FF // 128
TC = 512
NTC = (NSTEP * NTOK) // TC
NTILE = NSTEP * NTOK // 128


def host_consts():
    c = np.zeros((128, NCI), np.float32)
    c[:, C_EPS_RMS] = 1e-6
    c[:, C_INV_D] = 1.0 / D
    c[:, C_NEG_BIG] = -1e30
    c[:, C_EPS_LN] = 1e-5
    c[:, C_ONE] = 1.0
    c[:, C_NEG1] = -1.0
    c[:, C_NLRC0] = NEG_LRC0
    c[:, C_TWO] = 2.0
    c[:, C_NEGHALF] = -0.5
    return c


def build(n_cores=8):
    nc = bass.Bass("TRN2", target_bir_lowering=False)
    d_x = nc.dram_tensor("xT", [NSTEP, 128, DC * NTOK], F32, kind="ExternalInput").ap()
    d_tk = nc.dram_tensor("tkT", [D, D], F32, kind="ExternalInput").ap()
    d_tv = nc.dram_tensor("tvT", [D, D], F32, kind="ExternalInput").ap()
    d_tq = nc.dram_tensor("tqT", [D, D], F32, kind="ExternalInput").ap()
    d_w1 = nc.dram_tensor("w1T", [D, HI], F32, kind="ExternalInput").ap()
    d_w2 = nc.dram_tensor("w2T", [HI, D], F32, kind="ExternalInput").ap()
    d_b1 = nc.dram_tensor("b1c", [128, HC], F32, kind="ExternalInput").ap()
    d_b2 = nc.dram_tensor("b2c", [128, DC], F32, kind="ExternalInput").ap()
    d_gate = nc.dram_tensor("gate_wt", [D, E], F32, kind="ExternalInput").ap()
    d_rms = nc.dram_tensor("rms", [128, DC], F32, kind="ExternalInput").ap()
    d_consts = nc.dram_tensor("consts", [128, NCI], F32, kind="ExternalInput").ap()
    d_xnsp = nc.dram_tensor("xnsp", [NTILE, 128, DC * 128], F32,
                            kind="Internal").ap()
    d_hsp = nc.dram_tensor("hsp", [NTILE, 128, DC * 128], BF16,
                           kind="Internal").ap()
    d_w1t = nc.dram_tensor("w1t", [E * 7, 128, DC * 512], F32, kind="ExternalInput").ap()
    d_w3t = nc.dram_tensor("w3t", [E * 7, 128, DC * 512], F32, kind="ExternalInput").ap()
    d_w2t = nc.dram_tensor("w2t", [E, 128, NIB * D], F32, kind="ExternalInput").ap()
    d_w2p = nc.dram_tensor("w2pool", [E, 128, NIB * D], BF16, kind="Internal").ap()
    d_out = nc.dram_tensor("out", [NSTEP * NTOK, D], F32, kind="ExternalOutput").ap()
    cc_in = nc.dram_tensor("cc_in", [128, 4106], BF16, kind="Internal")
    cc_out = nc.dram_tensor("cc_out", [128, 4106], BF16, kind="Internal")

    ident_f = nc.inline_tensor(np.eye(128, dtype=np.float32), name="identf")
    ones_f = nc.inline_tensor(np.ones((1, 128), dtype=np.float32), name="onesf")

    ctx = contextlib.ExitStack()
    with ctx:
        sb = lambda name, shape, dt: ctx.enter_context(nc.sbuf_tensor(name, shape, dt))
        consts = sb("consts_sb", [128, NCI], F32)
        idf = sb("idf", [128, 128], F32)
        idb = sb("idb", [128, 128], BF16)
        rms_sb = sb("rmsw_sb", [128, DC], F32)
        ones_sb = sb("ones_sb", [1, 128], F32)
        cwT = sb("cwT", [8, NSTEP * NTOK], F32)
        mctx = contextlib.ExitStack()
        sbm = lambda name, shape, dt: mctx.enter_context(nc.sbuf_tensor(name, shape, dt))
        gate_rms = sbm("gate_rms", [128, DC * E], F32)
        tk_bf = sbm("tk_bf", [128, DC * D], BF16)
        tv_bf = sbm("tv_bf", [128, DC * D], BF16)
        tq_bf = sbm("tq_bf", [128, DC * D], BF16)
        w1t_f = sbm("w1t_f", [128, DC * HI], F32)
        w1t_bf = sbm("w1t_bf", [128, DC * HI], BF16)
        w2t_f = sbm("w2t_f", [128, HC * D], F32)
        w2t_bf = sbm("w2t_bf", [128, HC * D], BF16)
        w2n_bf = sbm("w2n_bf", [128, DC * HI], BF16)
        b1c = sbm("b1c_sb", [128, HC], F32)
        b2c = sbm("b2c_sb", [128, DC], F32)
        xbuf = sbm("xbuf", [128, DC * NTOK], F32)
        xbf = [sbm(f"xbf{p}", [128, DC * NTOK], BF16) for p in range(2)]
        zbuf = [sbm(f"zbuf{p}", [128, DC * NTOK], F32) for p in range(2)]
        zbf = [sbm(f"zbf{p}", [128, DC * NTOK], BF16) for p in range(2)]
        wstg = [sbm(f"wstg{p}", [128, 2048], BF16) for p in range(2)]
        qbf = sbm("qbf", [128, DC * NTOK], BF16)
        xq = sbm("xq", [128, DC * NTOK], F32)
        ebf = sbm("ebf", [128, DC * NTOK], BF16)
        hbf = sbm("hbf", [128, HC * NTOK], BF16)
        dgel = sbm("dgel", [128, HC * NTOK], F32)
        dla_bf = sbm("dla_bf", [128, HC * NTOK], BF16)
        ztok = sbm("ztok", [128, 2 * D], BF16)
        etok = sbm("etok", [128, 2 * D], BF16)
        htok = sbm("htok", [128, 2 * HI], BF16)
        dltok = sbm("dltok", [128, 2 * HI], BF16)
        h2bf = sbm("h2bf", [128, HC * NTOK], BF16)
        sT = sbm("sT", [128, DC * NTOK], F32)
        gbuf = sbm("gbuf", [128, 4106], BF16)
        xn_tok = sbm("xn_tok", [128, D], F32)
        hpre_tok = sbm("hpre_tok", [128, D], BF16)
        xnstage = sbm("xnstage", [128, DC * NTOK], F32)
        hstage = sbm("hstage", [128, DC * NTOK], BF16)
        lnws = sbm("lnws", [128, 16], F32)
        lnscr = sbm("lnscr", [128, 512], F32)
        lg = sbm("lg", [128, E], F32)
        gm1 = sbm("gm1", [128, E], F32)
        gm2 = sbm("gm2", [128, E], F32)
        gm3 = sbm("gm3", [128, E], F32)
        gws = sbm("gws", [128, 8], F32)
        cwtile = sbm("cwtile", [128, E], F32)

        sem = lambda name: ctx.enter_context(nc.semaphore(name))
        s_init = sem("s_init")
        s_initg = sem("s_initg")
        s_wc = [sem("s_wc0"), sem("s_wc1")]
        s_ws = [sem("s_ws0"), sem("s_ws1")]
        s_xt = sem("s_xt")
        s_xb = [sem("s_xb0"), sem("s_xb1")]
        s_prm = sem("s_prm")
        s_zm = sem("s_zm")
        s_zev = sem("s_zev")
        s_ym = sem("s_ym")
        s_yev = sem("s_yev")
        s_am = sem("s_am")
        s_hev = sem("s_hev")
        s_rm = sem("s_rm")
        s_eev = sem("s_eev")
        s_t1 = sem("s_t1")
        s_tv = sem("s_tv")
        s_dhm = sem("s_dhm")
        s_dhv = sem("s_dhv")
        s_dw2 = sem("s_dw2")
        s_gv2 = sem("s_gv2")
        s_dw1 = sem("s_dw1")
        s_qm = sem("s_qm")
        s_qev = sem("s_qev")
        s_gout = sem("s_gout")
        s_cc = sem("s_cc")
        s_gin = sem("s_gin")
        s_upd = sem("s_upd")
        s_ubf = sem("s_ubf")
        s_w2n = sem("s_w2n")
        s_w2nv = sem("s_w2nv")
        s_a2 = sem("s_a2")
        s_h2 = sem("s_h2")
        s_r2 = sem("s_r2")
        s_sT = sem("s_sT")
        s_lt = sem("s_lt")
        s_lacc = sem("s_lacc")
        s_lvv = sem("s_lvv")
        s_lvr = sem("s_lvr")
        s_le = sem("s_le")
        s_lsd = sem("s_lsd")
        s_li = sem("s_li")
        s_xnt = sem("s_xnt")
        s_rsd = sem("s_rsd")
        s_rin = sem("s_rin")
        s_hpt = sem("s_hpt")
        s_tb = sem("s_tb")
        s_stg = sem("s_stg")
        s_sp = [sem("s_sp0"), sem("s_sp1")]
        s_lgm = sem("s_lgm")
        s_gv = sem("s_gv")
        s_gs = sem("s_gs")
        s_cwt = sem("s_cwt")
        s_cwT = sem("s_cwT")
        s_cwsp = sem("s_cwsp")
        f_w13 = [[sem(f"f_w13_{p}_{q}") for q in range(4)] for p in range(2)]
        f_w2 = sem("f_w2")
        f_htc = sem("f_htc")
        f_atc = sem("f_atc")
        f_ag = sem("f_ag")
        f_sil = sem("f_sil")
        f_u = sem("f_u")
        f_eo = sem("f_eo")
        f_vx = sem("f_vx")
        f_acc = sem("f_acc")
        f_cwbm = sem("f_cwbm")
        f_cwbv = sem("f_cwbv")
        f_crow = sem("f_crow")
        f_lnt = sem("f_lnt")
        f_lnsqm = sem("f_lnsqm")
        f_lnv = sem("f_lnv")
        f_lnsd = sem("f_lnsd")
        f_lnw = sem("f_lnw")
        f_lni = sem("f_lni")
        f_lnn = sem("f_lnn")
        f_st = [sem(f"f_st{i}") for i in range(4)]
        f_acc2 = sem("f_acc2")

        block = ctx.enter_context(nc.Block())

        PBIG = mctx.enter_context(nc.psum_tensor("PBIG", [128, 2048], F32))
        PF4 = mctx.enter_context(nc.psum_tensor("PF4", [128, 512], F32))
        PF5 = mctx.enter_context(nc.psum_tensor("PF5", [128, 512], F32))
        PT = [mctx.enter_context(nc.psum_tensor(f"PT{i}", [128, 1024], BF16))
              for i in range(2)]

        def proj_tile(m):
            return PBIG[:, m * NTOK:(m + 1) * NTOK]

        @block.sync
        def _(s):
            s.dma_start(out=consts[:], in_=d_consts).then_inc(s_init, 16)
            s.dma_start(out=idf[:], in_=ident_f.ap()).then_inc(s_init, 16)
            s.dma_start(out=rms_sb[:], in_=d_rms).then_inc(s_init, 16)
            s.dma_start(out=b1c[:], in_=d_b1).then_inc(s_init, 16)
            s.dma_start(out=b2c[:], in_=d_b2).then_inc(s_init, 16)
            s.dma_start(out=gate_rms[:].rearrange("p (c e) -> p c e", c=DC),
                        in_=d_gate.rearrange("(c p) e -> p c e", p=128, c=DC)
                        ).then_inc(s_init, 16)
            s.dma_start(out=w1t_f[:].rearrange("p (c i) -> p c i", c=DC),
                        in_=d_w1.rearrange("(c p) i -> p c i", p=128, c=DC)
                        ).then_inc(s_init, 16)
            s.dma_start(out=w2t_f[:].rearrange("p (c h) -> p c h", c=HC),
                        in_=d_w2.rearrange("(c p) h -> p c h", p=128, c=HC)
                        ).then_inc(s_init, 16)
            s.dma_start(out=ones_sb[:], in_=ones_f.ap()).then_inc(s_init, 16)

            def xload(t):
                if t >= 1:
                    s.wait_ge(s_qev, 2 * t)
                s.dma_start(out=xbuf[:].rearrange("p (c n) -> p c n", c=DC),
                            in_=d_x[t].rearrange("p (c n) -> p c n", c=DC)
                            ).then_inc(s_xt, 16)

            xload(0)
            for t in range(T_RUN):
                if PHASE < 5:
                    continue
                s.wait_ge(s_gv2, 5 * (t + 1))
                s.dma_start(out=cc_in.ap(), in_=gbuf[:]).then_inc(s_gout, 16)
                s.wait_ge(s_cc, t + 1)
                s.dma_start(out=gbuf[:], in_=cc_out.ap()).then_inc(s_gin, 16)
                if t + 1 < T_RUN:
                    xload(t + 1)
                for jj in range(2):
                    gt = 2 * t + jj
                    s.wait_ge(s_stg, 2 * gt + 1)
                    s.dma_start(
                        out=d_xnsp[gt].rearrange("p (c n) -> p c n", c=DC),
                        in_=xnstage[:].rearrange("p (c n) -> p c n", c=DC)
                        [:, :, jj * 128:(jj + 1) * 128],
                    ).then_inc(s_sp[gt % 2], 16)
                    s.wait_ge(s_stg, 2 * gt + 2)
                    s.dma_start(
                        out=d_hsp[gt].rearrange("p (c n) -> p c n", c=DC),
                        in_=hstage[:].rearrange("p (c n) -> p c n", c=DC)
                        [:, :, jj * 128:(jj + 1) * 128],
                    ).then_inc(s_sp[gt % 2], 16)
            pass

        @block.gpsimd
        def _(g):
            g.dma_start(out=idb[:], in_=ident_f.ap()).then_inc(s_initg, 16)
            g.dma_start(out=tk_bf[:].rearrange("p (c h) -> p c h", c=DC),
                        in_=d_tk.rearrange("(c p) h -> p c h", p=128, c=DC)
                        ).then_inc(s_initg, 16)
            g.dma_start(out=tv_bf[:].rearrange("p (c h) -> p c h", c=DC),
                        in_=d_tv.rearrange("(c p) h -> p c h", p=128, c=DC)
                        ).then_inc(s_initg, 16)
            g.dma_start(out=tq_bf[:].rearrange("p (c h) -> p c h", c=DC),
                        in_=d_tq.rearrange("(c p) h -> p c h", p=128, c=DC)
                        ).then_inc(s_initg, 16)
            def xbload(t):
                if t >= 2:
                    g.wait_ge(s_qm, t - 1)
                g.dma_start(out=xbf[t % 2][:].rearrange("p (c n) -> p c n", c=DC),
                            in_=d_x[t].rearrange("p (c n) -> p c n", c=DC)
                            ).then_inc(s_xb[t % 2], 16)

            xbload(0)
            if T_RUN > 1:
                xbload(1)
            for t in range(T_RUN):
                if t + 2 < T_RUN:
                    xbload(t + 2)
                if PHASE < 5:
                    continue
                for j in range(14):
                    w = 14 * t + j
                    if w >= 2:
                        g.wait_ge(s_ws[w % 2], 16 * (w // 2))
                    g.dma_start(out=wstg[w % 2][:],
                                in_=d_w2t[t][:, j * 2048:(j + 1) * 2048]
                                ).then_inc(s_wc[w % 2], 16)
                    g.wait_ge(s_wc[w % 2], 16 * (w // 2 + 1))
                    g.dma_start(out=d_w2p[t][:, j * 2048:(j + 1) * 2048],
                                in_=wstg[w % 2][:]).then_inc(s_ws[w % 2], 16)
                g.wait_ge(s_gout, 16 * (t + 1))
                g.collective_compute(
                    "AllReduce", ALU.add, [list(range(n_cores))],
                    ins=[cc_in.ap().opt()], outs=[cc_out.ap().opt()],
                ).then_inc(s_cc)
                g.wait_ge(s_cc, t + 1)

        def emit_proj(t_, wsb, t):
            for m in range(DC):
                for k in range(DC):
                    ins = t_.matmul(proj_tile(m),
                                    wsb[:, k * D + m * 128: k * D + m * 128 + 128],
                                    xbf[t % 2][:, k * NTOK:(k + 1) * NTOK],
                                    start=(k == 0), stop=(k == DC - 1))
            return ins

        def emit_w2n_transposes(t_, step):
            for half in range(2):
                for k in range(4 * half, 4 * half + 4):
                    for c in range(HC):
                        idx = (k % 4) * 2 + c
                        ins = t_.transpose(
                            PT[half][:, idx * 128:idx * 128 + 128],
                            w2t_bf[:, c * D + k * 128: c * D + k * 128 + 128], idb[:])
                ins.then_inc(s_w2n, 1)

        def scalar_zevac(s, t):
            s.wait_ge(s_zev, 2 * t + 1)
            s.activation(zbf[t % 2][:], zbuf[t % 2][:], AF.Copy).then_inc(s_zev, 1)

        def vector_zevac(v, t):
            v.wait_ge(s_zm, t + 1)
            if t >= 1:
                v.wait_ge(s_zev, 2 * t)
            if t >= 2:
                v.wait_ge(s_eev, t - 1)
                v.wait_ge(s_t1, 6 * (t - 2) + 2)
            v.tensor_copy(zbuf[t % 2][:], PBIG[:]).then_inc(s_zev, 1)

        @block.tensor
        def _(t_):
            t_.wait_ge(s_init, 9 * 16)
            t_.wait_ge(s_initg, 4 * 16)
            t_.wait_ge(s_prm, 2)
            emit_w2n_transposes(t_, -1)
            t_.wait_ge(s_xb[0], 16)
            emit_proj(t_, tk_bf, 0).then_inc(s_zm, 1)
            for t in range(T_RUN):
                if PHASE < 2:
                    continue
                t_.wait_ge(s_zev, 1 if t == 0 else min(2 * (t + 1) + 1, 2 * T_RUN))
                if t >= 1:
                    t_.wait_ge(s_gv, NGV * 2 * t)
                    t_.wait_ge(s_cwT, 2 * t)
                emit_proj(t_, tv_bf, t).then_inc(s_ym, 1)
                if t >= 1:
                    t_.wait_ge(s_lacc, 8 * t)
                    t_.wait_ge(s_xnt, 2 * t)
                t_.wait_ge(s_zev, 2 * (t + 1))
                for m in range(HC):
                    for k in range(DC):
                        ins = t_.matmul(PF4[:, m * NTOK:(m + 1) * NTOK],
                                        w1t_bf[:, k * HI + m * 128: k * HI + m * 128 + 128],
                                        zbf[t % 2][:, k * NTOK:(k + 1) * NTOK],
                                        start=(k == 0), stop=(k == DC - 1))
                ins.then_inc(s_am, 1)
                if PHASE < 3:
                    continue
                t_.wait_ge(s_yev, t + 1)
                t_.wait_ge(s_hev, 2 * (t + 1))
                for m in range(DC):
                    for k in range(HC):
                        ins = t_.matmul(proj_tile(m),
                                        w2t_bf[:, k * D + m * 128: k * D + m * 128 + 128],
                                        hbf[:, k * NTOK:(k + 1) * NTOK],
                                        start=(k == 0), stop=(k == HC - 1))
                ins.then_inc(s_rm, 1)
                t_.wait_ge(s_zev, 2 * (t + 1))
                t_.wait_ge(s_w2nv, 2 * (t + 1))
                if t >= 1:
                    t_.wait_ge(s_tb, 4 * t)
                    t_.wait_ge(s_stg, 4 * t)
                for half in range(2):
                    for idx in range(8):
                        gidx = idx + half * 8
                        tc4, k = gidx // 8, gidx % 8
                        ins = t_.transpose(PT[half][:, idx * 128:(idx + 1) * 128],
                                           zbf[t % 2][:, k * NTOK + tc4 * 128:
                                                      k * NTOK + tc4 * 128 + 128], idb[:])
                    ins.then_inc(s_t1, 1)
                t_.wait_ge(s_eev, t + 1)
                t_.wait_ge(s_tv, 6 * t + 2)
                for half in range(2):
                    for idx in range(8):
                        gidx = idx + half * 8
                        tc4, k = gidx // 8, gidx % 8
                        ins = t_.transpose(PT[half][:, idx * 128:(idx + 1) * 128],
                                           ebf[:, k * NTOK + tc4 * 128:
                                               k * NTOK + tc4 * 128 + 128], idb[:])
                    ins.then_inc(s_t1, 1)
                t_.wait_ge(s_tv, 6 * t + 3)
                for idx in range(4):
                    tc4, k = idx // HC, idx % HC
                    ins = t_.transpose(PT[0][:, idx * 128:(idx + 1) * 128],
                                       hbf[:, k * NTOK + tc4 * 128:
                                           k * NTOK + tc4 * 128 + 128], idb[:])
                ins.then_inc(s_t1, 1)
                if PHASE < 4:
                    continue
                for m in range(HC):
                    for k in range(DC):
                        ins = t_.matmul(PF4[:, m * NTOK:(m + 1) * NTOK],
                                        w2n_bf[:, k * HI + m * 128: k * HI + m * 128 + 128],
                                        ebf[:, k * NTOK:(k + 1) * NTOK],
                                        start=(k == 0), stop=(k == DC - 1))
                ins.then_inc(s_dhm, 1)
                t_.wait_ge(s_dhv, 2 * t + 1)
                t_.wait_ge(s_tv, 6 * t + 4)
                for idx in range(4):
                    tc4, k = idx // HC, idx % HC
                    ins = t_.transpose(PT[1][:, idx * 128:(idx + 1) * 128],
                                       dla_bf[:, k * NTOK + tc4 * 128:
                                              k * NTOK + tc4 * 128 + 128], idb[:])
                ins.then_inc(s_t1, 1)
                t_.wait_ge(s_tv, 6 * t + 5)
                for mi in range(HC):
                    for nh in range(2):
                        q4 = mi * 2 + nh
                        if q4 > 0:
                            t_.wait_ge(s_gv2, 5 * t + q4)
                        elif t >= 1:
                            t_.wait_ge(s_lacc, 8 * t)
                            t_.wait_ge(s_xnt, 2 * t)
                        for k in range(2):
                            ins = t_.matmul(PF5[:],
                                            htok[:, k * HI + mi * 128: k * HI + mi * 128 + 128],
                                            etok[:, k * D + nh * 512: k * D + nh * 512 + 512],
                                            start=(k == 0), stop=(k == 1))
                        ins.then_inc(s_dw2, 1)
                t_.wait_ge(s_eev, t + 1)
                t_.wait_ge(s_tv, 6 * t + 6)
                for m in range(DC):
                    for k in range(2):
                        ins = t_.matmul(proj_tile(m),
                                        ztok[:, k * D + m * 128: k * D + m * 128 + 128],
                                        dltok[:, k * HI: k * HI + NTOK],
                                        start=(k == 0), stop=(k == 1))
                ins.then_inc(s_dw1, 1)
                if PHASE < 5:
                    continue
                t_.wait_ge(s_gv2, 5 * (t + 1))
                emit_proj(t_, tq_bf, t).then_inc(s_qm, 1)
                if t == 0:
                    t_.wait_ge(s_qev, 2)
                    t_.wait_ge(s_xb[1], 16)
                    emit_proj(t_, tk_bf, 1).then_inc(s_zm, 1)
                if t + 2 < T_RUN:
                    if t == 0:
                        t_.wait_ge(s_zev, 3)
                    else:
                        t_.wait_ge(s_qev, 2 * (t + 1))
                    t_.wait_ge(s_xb[t % 2], 16 * ((t + 2) // 2 + 1))
                    emit_proj(t_, tk_bf, t + 2).then_inc(s_zm, 1)
                t_.wait_ge(s_ubf, 2 * t + 1)
                if PHASE < 6:
                    continue
                t_.wait_ge(s_dhv, 2 * t + 1)
                for m in range(HC):
                    for k in range(DC):
                        ins = t_.matmul(PF4[:, m * NTOK:(m + 1) * NTOK],
                                        w1t_bf[:, k * HI + m * 128: k * HI + m * 128 + 128],
                                        qbf[:, k * NTOK:(k + 1) * NTOK],
                                        start=(k == 0), stop=(k == DC - 1))
                ins.then_inc(s_a2, 1)
                t_.wait_ge(s_ubf, 2 * (t + 1))
                t_.wait_ge(s_tv, 6 * (t + 1))
                emit_w2n_transposes(t_, t)
                if t + 2 < T_RUN:
                    t_.wait_ge(s_zev, 2 * (t + 2) + 1)
                else:
                    t_.wait_ge(s_qev, 2 * (t + 1))
                t_.wait_ge(s_h2, t + 1)
                for m in range(DC):
                    for k in range(HC):
                        ins = t_.matmul(proj_tile(m),
                                        w2t_bf[:, k * D + m * 128: k * D + m * 128 + 128],
                                        h2bf[:, k * NTOK:(k + 1) * NTOK],
                                        start=(k == 0), stop=(k == HC - 1))
                ins.then_inc(s_r2, 1)
                if PHASE < 7:
                    continue
                for jj in range(2):
                    gt = 2 * t + jj
                    t_.wait_ge(s_sT, t + 1)
                    if jj == 0:
                        t_.wait_ge(s_h2, t + 1)
                    else:
                        t_.wait_ge(s_lacc, 4 * gt)
                        t_.wait_ge(s_xnt, gt)
                    for k in range(DC):
                        dst = (PF4 if k < 4 else PF5)[:, (k % 4) * 128:(k % 4) * 128 + 128]
                        ins = t_.transpose(dst, sT[:, k * NTOK + jj * 128:
                                                   k * NTOK + jj * 128 + 128], idf[:])
                    ins.then_inc(s_lt, 1)
                    t_.wait_ge(s_xnt, gt + 1)
                    if gt >= 1:
                        t_.wait_ge(s_stg, 2 * gt - 1)
                    for k in range(DC):
                        dst = PBIG[:, k * 128:(k + 1) * 128]
                        ins = t_.transpose(dst, xn_tok[:, k * 128:(k + 1) * 128], idf[:])
                    ins.then_inc(s_tb, 1)
                    t_.wait_ge(s_hpt, gt + 1)
                    t_.wait_ge(s_w2nv, 2 * (t + 2))
                    if gt >= 1:
                        t_.wait_ge(s_stg, 2 * gt)
                    for k in range(DC):
                        ins = t_.transpose(PT[0][:, k * 128:(k + 1) * 128],
                                           hpre_tok[:, k * 128:(k + 1) * 128], idb[:])
                    ins.then_inc(s_tb, 1)
                    t_.wait_ge(s_stg, 2 * gt + 1)
                    t_.wait_ge(s_gs, 1)
                    if gt >= 1:
                        t_.wait_ge(s_gv, NGV * (gt - 1) + 1)
                    for k in range(DC):
                        ins = t_.matmul(PBIG[:, 1024:1032],
                                        xnstage[:, k * NTOK + jj * 128:
                                                k * NTOK + jj * 128 + 128],
                                        gate_rms[:, k * E:(k + 1) * E],
                                        start=(k == 0), stop=(k == DC - 1))
                    ins.then_inc(s_lgm, 1)
                    t_.wait_ge(s_gv, NGV * (gt + 1))
                    if gt >= 1:
                        t_.wait_ge(s_cwT, gt)
                    t_.transpose(PBIG[0:8, 1536:1664], cwtile[:], idf[:]).then_inc(s_cwt, 1)

        @block.scalar
        def _(s):
            s.wait_ge(s_init, 9 * 16)
            for k in range(DC):
                ins = s.activation(gate_rms[:, k * E:(k + 1) * E],
                                   gate_rms[:, k * E:(k + 1) * E],
                                   AF.Copy, scale=rms_sb[:, k:k + 1])
            ins.then_inc(s_gs, 1)
            s.activation(w1t_bf[:], w1t_f[:], AF.Copy).then_inc(s_prm, 1)
            s.activation(w2t_bf[:], w2t_f[:], AF.Copy).then_inc(s_prm, 1)
            scalar_zevac(s, 0)
            for t in range(T_RUN):
                if PHASE < 2:
                    continue
                s.wait_ge(s_am, t + 1)
                if t >= 1:
                    s.wait_ge(s_dhv, 2 * (t - 1) + 1)
                    s.wait_ge(s_t1, 6 * (t - 1) + 5)
                for m in range(HC):
                    s.activation(hbf[:, m * NTOK:(m + 1) * NTOK],
                                 PF4[:, m * NTOK:(m + 1) * NTOK], AF.Gelu,
                                 bias=b1c[:, m:m + 1])
                for m in range(HC):
                    ins = s.activation(dgel[:, m * NTOK:(m + 1) * NTOK],
                                       PF4[:, m * NTOK:(m + 1) * NTOK],
                                       AF.Derivative_Gelu, bias=b1c[:, m:m + 1])
                ins.then_inc(s_hev, 2)
                if PHASE < 3:
                    continue

                if PHASE < 4:
                    continue

                if PHASE < 5:
                    continue
                s.wait_ge(s_qm, t + 1)
                s.wait_ge(s_qev, 2 * t + 1)
                if t >= 1:
                    s.wait_ge(s_a2, t)
                s.activation(qbf[:], PBIG[:], AF.Copy).then_inc(s_qev, 1)
                if t == 0:
                    scalar_zevac(s, 1)
                if t + 2 < T_RUN:
                    scalar_zevac(s, t + 2)
                s.wait_ge(s_upd, 4 * t + 1)
                s.activation(w1t_bf[:], w1t_f[:], AF.Copy).then_inc(s_ubf, 1)
                s.wait_ge(s_upd, 4 * t + 2)
                s.activation(w2t_bf[:], w2t_f[:], AF.Copy).then_inc(s_ubf, 1)
                if PHASE < 6:
                    continue
                s.wait_ge(s_upd, 4 * t + 3)
                s.wait_ge(s_a2, t + 1)
                for m in range(HC):
                    ins = s.activation(h2bf[:, m * NTOK:(m + 1) * NTOK],
                                       PF4[:, m * NTOK:(m + 1) * NTOK], AF.Gelu,
                                       bias=b1c[:, m:m + 1])
                ins.then_inc(s_h2, 1)
                if PHASE < 7:
                    continue
                for jj in range(2):
                    gt = 2 * t + jj
                    s.wait_ge(s_lt, gt + 1)
                    if gt >= 1:
                        s.wait_ge(s_lvv, gt)
                    s.activation(lnscr[:], PF4[:], AF.Copy,
                                 accum_out=lnws[:, 0:1]).then_inc(s_lacc, 1)
                    s.wait_ge(s_lacc, 4 * gt + 1)
                    s.activation(lnscr[:], PF5[:], AF.Copy,
                                 accum_out=lnws[:, 1:2]).then_inc(s_lacc, 1)
                    s.wait_ge(s_lacc, 4 * gt + 2)
                    s.activation(lnscr[:], PF4[:], AF.Square,
                                 accum_out=lnws[:, 2:3]).then_inc(s_lacc, 1)
                    s.wait_ge(s_lacc, 4 * gt + 3)
                    s.activation(lnscr[:], PF5[:], AF.Square,
                                 accum_out=lnws[:, 3:4]).then_inc(s_lacc, 1)
                    s.wait_ge(s_lvv, gt + 1)
                    s.activation(lnws[:, 8:9], lnws[:, 7:8], AF.Ln,
                                 bias=consts[:, C_EPS_LN:C_EPS_LN + 1]).then_inc(s_lsd, 1)
                    s.wait_ge(s_lsd, 2 * gt + 1)
                    s.activation(lnws[:, 9:10], lnws[:, 8:9], AF.Exp,
                                 scale=consts[:, C_NEGHALF:C_NEGHALF + 1]).then_inc(s_lsd, 1)
                    s.wait_ge(s_li, gt + 1)
                    if gt >= 1:
                        s.wait_ge(s_tb, 2 * gt - 1)
                        s.wait_ge(s_hpt, gt)
                    s.activation(xn_tok[:, 0:512], PF4[:], AF.Identity,
                                 scale=lnws[:, 9:10], bias=lnws[:, 10:11])
                    ins = s.activation(xn_tok[:, 512:1024], PF5[:], AF.Identity,
                                       scale=lnws[:, 9:10], bias=lnws[:, 10:11])
                    ins.then_inc(s_xnt, 1)
                    s.wait_ge(s_lvr, gt + 1)
                    s.activation(lnws[:, 14:15], lnws[:, 13:14], AF.Ln,
                                 bias=consts[:, C_EPS_RMS:C_EPS_RMS + 1]).then_inc(s_rsd, 1)
                    s.wait_ge(s_rsd, 2 * gt + 1)
                    s.activation(lnws[:, 15:16], lnws[:, 14:15], AF.Exp,
                                 scale=consts[:, C_NEGHALF:C_NEGHALF + 1]).then_inc(s_rsd, 1)
                    s.wait_ge(s_rsd, 2 * (gt + 1))
                    if gt >= 1:
                        s.wait_ge(s_tb, 2 * gt)
                    s.activation(hpre_tok[:], xn_tok[:], AF.Copy,
                                 scale=lnws[:, 15:16]).then_inc(s_hpt, 1)
                    s.wait_ge(s_gv, NGV * gt + 2)
                    s.activation(gm3[:], lg[:], AF.Exp, bias=gws[:, 0:1]).then_inc(s_gs, 1)
                    s.wait_ge(s_gv, NGV * gt + 7)
                    s.activation(gws[:, 3:4], gws[:, 2:3], AF.Exp).then_inc(s_gs, 1)

        @block.vector
        def _(v):
            self_edge = [0]
            for half in range(2):
                v.wait_ge(s_w2n, half + 1)
                v.tensor_copy(w2n_bf[:, half * 1024:(half + 1) * 1024],
                              PT[half][:]).then_inc(s_w2nv, 1)
            vector_zevac(v, 0)
            for t in range(T_RUN):
                if PHASE < 2:
                    continue
                v.wait_ge(s_ym, t + 1)
                v.wait_ge(s_zev, 2 * (t + 1))
                for m in range(DC):
                    ins = v.scalar_tensor_tensor(zbuf[t % 2][:, m * NTOK:(m + 1) * NTOK],
                                                 zbuf[t % 2][:, m * NTOK:(m + 1) * NTOK],
                                                 b2c[:, m:m + 1], proj_tile(m),
                                                 ALU.add, ALU.subtract)
                ins.then_inc(s_yev, 1)
                if PHASE < 3:
                    continue
                v.wait_ge(s_rm, t + 1)
                v.wait_ge(s_yev, t + 1)
                if t >= 1:
                    v.wait_ge(s_dhm, t)
                    v.wait_ge(s_t1, 6 * (t - 1) + 4)
                v.tensor_tensor(ebf[:], zbuf[t % 2][:], PBIG[:],
                                ALU.add).then_inc(s_eev, 1)
                v.wait_ge(s_t1, 6 * t + 1)
                v.tensor_copy(ztok[:, 0:1024], PT[0][:]).then_inc(s_tv, 1)
                v.wait_ge(s_t1, 6 * t + 2)
                v.tensor_copy(ztok[:, 1024:2048], PT[1][:]).then_inc(s_tv, 1)
                v.wait_ge(s_t1, 6 * t + 3)
                v.tensor_copy(etok[:, 0:1024], PT[0][:]).then_inc(s_tv, 1)
                v.wait_ge(s_t1, 6 * t + 4)
                v.tensor_copy(etok[:, 1024:2048], PT[1][:]).then_inc(s_tv, 1)
                v.wait_ge(s_t1, 6 * t + 5)
                v.tensor_copy(htok[:], PT[0][:, 0:512]).then_inc(s_tv, 1)
                if PHASE < 4:
                    continue
                v.wait_ge(s_dhm, t + 1)
                if t >= 1:
                    v.wait_ge(s_t1, 6 * (t - 1) + 6)
                for m in range(HC):
                    ins = v.tensor_tensor(dla_bf[:, m * NTOK:(m + 1) * NTOK],
                                          dgel[:, m * NTOK:(m + 1) * NTOK],
                                          PF4[:, m * NTOK:(m + 1) * NTOK], ALU.mult)
                ins.then_inc(s_dhv, 1)
                v.wait_ge(s_dhv, 2 * t + 1)
                v.reduce_sum(lnscr[:, 16:17], dla_bf[:, 0:NTOK],
                             axis=mybir.AxisListType.X)
                v.reduce_sum(lnscr[:, 17:18], dla_bf[:, NTOK:2 * NTOK],
                             axis=mybir.AxisListType.X).then_inc(s_le, 1)
                self_edge[0] += 1
                v.wait_ge(s_le, self_edge[0])
                ins = v.tensor_copy(gbuf[:, 4096:4098], lnscr[:, 16:18])
                ins.then_inc(s_dhv, 1)
                v.wait_ge(s_t1, 6 * t + 6)
                v.tensor_copy(dltok[:], PT[1][:, 0:512]).then_inc(s_tv, 1)
                for q4 in range(4):
                    mi, nh = q4 // 2, q4 % 2
                    v.wait_ge(s_dw2, 4 * t + q4 + 1)
                    ins = v.tensor_copy(
                        gbuf[:, 2048 + mi * 1024 + nh * 512:
                             2048 + mi * 1024 + nh * 512 + 512], PF5[:])
                    ins.then_inc(s_gv2, 1)
                for m in range(DC):
                    ins = v.reduce_sum(lnscr[:, 20 + m:21 + m],
                                       ebf[:, m * NTOK:(m + 1) * NTOK],
                                       axis=mybir.AxisListType.X)
                ins.then_inc(s_le, 1)
                self_edge[0] += 1
                v.wait_ge(s_le, self_edge[0])
                v.tensor_copy(gbuf[:, 4098:4106], lnscr[:, 20:28])
                v.wait_ge(s_dw1, t + 1)
                v.tensor_copy(gbuf[:, 0:2048], PBIG[:]).then_inc(s_gv2, 1)
                if PHASE < 5:
                    continue
                v.wait_ge(s_qm, t + 1)
                v.wait_ge(s_xt, 16 * (t + 1))
                if t >= 1:
                    v.wait_ge(s_sT, t)
                v.tensor_tensor(xq[:], PBIG[:], xbuf[:], ALU.add).then_inc(s_qev, 1)
                if t == 0:
                    vector_zevac(v, 1)
                if t + 2 < T_RUN:
                    vector_zevac(v, t + 2)
                v.wait_ge(s_gin, 16 * (t + 1))
                v.wait_ge(s_ubf, 2 * t)
                v.scalar_tensor_tensor(w1t_f[:], gbuf[:, 0:2048],
                                       consts[:, C_NLRC0:C_NLRC0 + 1], w1t_f[:],
                                       ALU.mult, ALU.add).then_inc(s_upd, 1)
                v.scalar_tensor_tensor(w2t_f[:], gbuf[:, 2048:4096],
                                       consts[:, C_NLRC0:C_NLRC0 + 1], w2t_f[:],
                                       ALU.mult, ALU.add).then_inc(s_upd, 1)
                v.scalar_tensor_tensor(b1c[:], gbuf[:, 4096:4098],
                                       consts[:, C_NLRC0:C_NLRC0 + 1], b1c[:],
                                       ALU.mult, ALU.add).then_inc(s_upd, 1)
                v.scalar_tensor_tensor(b2c[:], gbuf[:, 4098:4106],
                                       consts[:, C_NLRC0:C_NLRC0 + 1], b2c[:],
                                       ALU.mult, ALU.add).then_inc(s_upd, 1)
                for half in range(2):
                    v.wait_ge(s_w2n, 2 * (t + 1) + half + 1)
                    v.tensor_copy(w2n_bf[:, half * 1024:(half + 1) * 1024],
                                  PT[half][:]).then_inc(s_w2nv, 1)
                if PHASE < 6:
                    continue
                v.wait_ge(s_r2, t + 1)
                v.wait_ge(s_upd, 4 * (t + 1))
                v.wait_ge(s_qev, 2 * (t + 1))
                for m in range(DC):
                    ins = v.scalar_tensor_tensor(sT[:, m * NTOK:(m + 1) * NTOK],
                                                 proj_tile(m), b2c[:, m:m + 1],
                                                 xq[:, m * NTOK:(m + 1) * NTOK],
                                                 ALU.add, ALU.add)
                ins.then_inc(s_sT, 1)
                if PHASE < 7:
                    continue
                for jj in range(2):
                    gt = 2 * t + jj

                    def vtick(ins):
                        self_edge[0] += 1
                        ins.then_inc(s_le, 1)

                    def vedge():
                        v.wait_ge(s_le, self_edge[0])

                    v.wait_ge(s_lacc, 4 * (gt + 1))
                    vtick(v.tensor_tensor(lnws[:, 4:5], lnws[:, 0:1], lnws[:, 1:2],
                                          ALU.add))
                    vedge()
                    vtick(v.tensor_scalar(lnws[:, 5:6], lnws[:, 4:5],
                                          consts[:, C_INV_D:C_INV_D + 1], None,
                                          ALU.mult))
                    vtick(v.tensor_tensor(lnws[:, 6:7], lnws[:, 2:3], lnws[:, 3:4],
                                          ALU.add))
                    vedge()
                    vtick(v.tensor_scalar(lnws[:, 6:7], lnws[:, 6:7],
                                          consts[:, C_INV_D:C_INV_D + 1], None,
                                          ALU.mult))
                    vtick(v.tensor_tensor(lnscr[:, 0:1], lnws[:, 5:6], lnws[:, 5:6],
                                          ALU.mult))
                    vedge()
                    v.tensor_tensor(lnws[:, 7:8], lnws[:, 6:7], lnscr[:, 0:1],
                                    ALU.subtract).then_inc(s_lvv, 1)
                    v.wait_ge(s_lsd, 2 * (gt + 1))
                    vtick(v.tensor_tensor(lnscr[:, 1:2], lnws[:, 5:6], lnws[:, 9:10],
                                          ALU.mult))
                    vedge()
                    v.tensor_scalar(lnws[:, 10:11], lnscr[:, 1:2],
                                    consts[:, C_NEG1:C_NEG1 + 1], None,
                                    ALU.mult).then_inc(s_li, 1)
                    if gt >= 1:
                        v.wait_ge(s_rsd, 2 * gt - 1)
                    vtick(v.tensor_tensor(lnws[:, 11:12], lnws[:, 9:10], lnws[:, 9:10],
                                          ALU.mult))
                    vedge()
                    v.tensor_tensor(lnws[:, 13:14], lnws[:, 7:8], lnws[:, 11:12],
                                    ALU.mult).then_inc(s_lvr, 1)
                    v.wait_ge(s_tb, 2 * gt + 1)
                    if gt >= 1:
                        v.wait_ge(s_sp[(gt - 1) % 2], 32 * ((gt - 1) // 2 + 1))
                    if gt >= 2:
                        v.wait_ge(s_sp[gt % 2], 32 * (gt // 2))
                    dst = xnstage[:].rearrange("p (k t) -> p k t", k=DC)[
                        :, :, jj * 128:jj * 128 + 128]
                    ins = v.tensor_copy(dst, PBIG[:, 0:1024]
                                        .rearrange("p (k t) -> p k t", k=DC))
                    ins.then_inc(s_stg, 1)
                    v.wait_ge(s_tb, 2 * gt + 2)
                    for k in range(DC):
                        ins = v.tensor_scalar(
                            hstage[:, k * NTOK + jj * 128: k * NTOK + jj * 128 + 128],
                            PT[0][:, k * 128:(k + 1) * 128],
                            rms_sb[:, k:k + 1], None, ALU.mult)
                    ins.then_inc(s_stg, 1)
                    v.wait_ge(s_lgm, gt + 1)
                    v.tensor_copy(lg[:], PBIG[:, 1024:1032]).then_inc(s_gv, 1)
                    v.wait_ge(s_gv, NGV * gt + 1)
                    v.reduce_max(gws[:, 0:1], lg[:], axis=mybir.AxisListType.X,
                                 negate=True).then_inc(s_gv, 1)
                    v.wait_ge(s_gv, NGV * gt + 2)
                    v.tensor_scalar(gm1[:], lg[:], gws[:, 0:1],
                                    consts[:, C_ZERO:C_ZERO + 1],
                                    ALU.add, ALU.is_equal).then_inc(s_gv, 1)
                    v.wait_ge(s_gv, NGV * gt + 3)
                    v.scalar_tensor_tensor(gm2[:], gm1[:],
                                           consts[:, C_NEG_BIG:C_NEG_BIG + 1],
                                           lg[:], ALU.mult, ALU.add).then_inc(s_gv, 1)
                    v.wait_ge(s_gv, NGV * gt + 4)
                    v.reduce_max(gws[:, 1:2], gm2[:], axis=mybir.AxisListType.X,
                                 negate=True).then_inc(s_gv, 1)
                    v.wait_ge(s_gv, NGV * gt + 5)
                    v.tensor_scalar(gm2[:], lg[:], gws[:, 1:2],
                                    consts[:, C_ZERO:C_ZERO + 1],
                                    ALU.add, ALU.is_ge).then_inc(s_gv, 1)
                    v.tensor_tensor(gws[:, 2:3], gws[:, 0:1], gws[:, 1:2],
                                    ALU.subtract).then_inc(s_gv, 1)
                    v.wait_ge(s_gs, 2 * gt + 3)
                    v.tensor_scalar(gws[:, 4:5], gws[:, 3:4],
                                    consts[:, C_ONE:C_ONE + 1], None,
                                    ALU.add).then_inc(s_gv, 1)
                    v.wait_ge(s_gv, NGV * gt + 8)
                    v.reciprocal(gws[:, 5:6], gws[:, 4:5]).then_inc(s_gv, 1)
                    v.wait_ge(s_gs, 2 * gt + 2)
                    v.tensor_tensor(gm1[:], gm2[:], gm3[:], ALU.mult).then_inc(s_gv, 1)
                    v.wait_ge(s_gv, NGV * gt + 10)
                    if gt >= 1:
                        v.wait_ge(s_cwt, gt)
                    v.tensor_scalar(cwtile[:], gm1[:], gws[:, 5:6], None,
                                    ALU.mult).then_inc(s_gv, 1)
                    v.wait_ge(s_cwt, gt + 1)
                    v.tensor_copy(cwT[0:8, gt * 128:(gt + 1) * 128],
                                  PBIG[0:8, 1536:1664]).then_inc(s_cwT, 1)


        mctx.close()

        fsb = lambda name, shape, dt: ctx.enter_context(nc.sbuf_tensor(name, shape, dt))
        cwrow = fsb("cwrow", [1, TC], F32)
        cwb = fsb("cwb", [128, TC], F32)
        hbuf2 = fsb("hbuf2", [128, DC * TC], BF16)
        acc = fsb("acc", [128, DC * TC], F32)
        ubuf = fsb("ubuf", [128, NIB * TC], BF16)
        w2sb = fsb("w2sb", [128, NIB * D], BF16)
        wb1 = [fsb(f"wb1_{p}", [128, DC * 512], BF16) for p in range(2)]
        wb3 = [fsb(f"wb3_{p}", [128, DC * 512], BF16) for p in range(2)]
        stmp = [fsb(f"stmp_{p}", [128, TC], F32) for p in range(2)]
        vtmp = fsb("vtmp", [128, TC], F32)
        outsb = [fsb(f"outsb_{p}", [128, D], F32) for p in range(4)]
        lnsq2 = fsb("lnsq2", [128, D], F32)
        lnws2 = fsb("lnws2", [128, 10], F32)

        pa = [ctx.enter_context(nc.psum_tensor(f"pa{p}", [128, TC], F32)) for p in range(2)]
        pg = [ctx.enter_context(nc.psum_tensor(f"pg{p}", [128, TC], F32)) for p in range(2)]
        peo = [ctx.enter_context(nc.psum_tensor(f"peo{p}", [128, TC], F32)) for p in range(2)]
        pln = ctx.enter_context(nc.psum_tensor("pln", [128, D], F32))

        def GE(tc, e):
            return tc * E + e

        @block.gpsimd
        def _(g):
            g.wait_ge(s_cwT, NTILE)
            g.wait_ge(s_sp[0], 32 * NSTEP)
            g.wait_ge(s_sp[1], 32 * NSTEP)
            g.wait_ge(s_ws[0], 16 * 56)
            g.wait_ge(s_ws[1], 16 * 56)
            for tc in range(NTC):
                for e in range(E):
                    ge = GE(tc, e)
                    for ic in range(7):
                        gch = ge * 7 + ic
                        if gch >= 2:
                            g.wait_ge(f_ag, 2 * (4 * (gch - 2) + 4))
                        for q in range(4):
                            g.dma_start(out=wb1[gch % 2][:, q * 1024:(q + 1) * 1024],
                                        in_=d_w1t[e * 7 + ic][:, q * 1024:(q + 1) * 1024]
                                        ).then_inc(f_w13[gch % 2][q], 16)
                            g.dma_start(out=wb3[gch % 2][:, q * 1024:(q + 1) * 1024],
                                        in_=d_w3t[e * 7 + ic][:, q * 1024:(q + 1) * 1024]
                                        ).then_inc(f_w13[gch % 2][q], 16)
                        if 1 <= ic <= 4:
                            r0 = (ic - 1) * 7
                            if r0 == 0:
                                g.wait_ge(f_eo, 8 * ge)
                                if tc == 0:
                                    g.wait_ge(s_ws[0], 112 * (e + 1))
                                    g.wait_ge(s_ws[1], 112 * (e + 1))
                            for piece in range(r0, r0 + 7):
                                g.dma_start(out=w2sb[:, piece * D:(piece + 1) * D],
                                            in_=d_w2p[e][:, piece * D:(piece + 1) * D]
                                            ).then_inc(f_w2, 16)

        @block.sync
        def _(s):
            s.wait_ge(s_cwT, NTILE)
            s.wait_ge(s_sp[0], 32 * NSTEP)
            s.wait_ge(s_sp[1], 32 * NSTEP)

            def load_tc(tc):
                s.wait_ge(s_sp[0], 32 * 2 * (tc + 1))
                s.wait_ge(s_sp[1], 32 * 2 * (tc + 1))
                if tc >= 1:
                    s.wait_ge(f_ag, 2 * 28 * 8 * tc)
                for q in range(4):
                    s.dma_start(out=hbuf2[:].rearrange("p (c t) -> p c t", c=DC)
                                [:, :, q * 128:(q + 1) * 128],
                                in_=d_hsp[tc * 4 + q].rearrange("p (c n) -> p c n", c=DC)
                                ).then_inc(f_htc, 16)
                if tc >= 1:
                    s.wait_ge(f_lnt, 4 * tc)
                for q in range(4):
                    s.dma_start(out=acc[:].rearrange("p (c t) -> p c t", c=DC)
                                [:, :, q * 128:(q + 1) * 128],
                                in_=d_xnsp[tc * 4 + q].rearrange("p (c n) -> p c n", c=DC)
                                ).then_inc(f_atc, 16)

            def cwrows(tc):
                for e in range(E):
                    ge = GE(tc, e)
                    s.wait_ge(s_cwT, 4 * (tc + 1))
                    s.wait_ge(f_cwbm, ge)
                    s.dma_start(out=cwrow[:],
                                in_=cwT[e:e + 1, tc * TC:(tc + 1) * TC]).then_inc(f_crow, 16)

            load_tc(0)
            cwrows(0)
            for tc in range(NTC):
                if tc + 1 < NTC:
                    load_tc(tc + 1)
                    cwrows(tc + 1)
                for jj in range(4):
                    gt = tc * 4 + jj
                    s.wait_ge(f_lnn, gt + 1)
                    s.dma_start(out=d_out[gt * 128:(gt + 1) * 128, :],
                                in_=outsb[gt % 4][:]).then_inc(f_st[gt % 4], 16)
            for q in range(4):
                s.wait_ge(f_st[q], 16 * (NTILE // 4))

        @block.tensor
        def _(t_):
            t_.wait_ge(s_cwT, NTILE)
            for tc in range(NTC):
                t_.wait_ge(f_htc, 64 * (tc + 1))
                for e in range(E):
                    ge = GE(tc, e)
                    t_.wait_ge(f_crow, 16 * (ge + 1))
                    if ge >= 1:
                        t_.wait_ge(f_acc, 8 * (ge - 1) + 7)
                    t_.matmul(peo[0][:], ones_sb[:], cwrow[:],
                              start=True, stop=True).then_inc(f_cwbm, 1)
                    for ib in range(NIB):
                        gib = ge * NIB + ib
                        gch = ge * 7 + ib // 4
                        t_.wait_ge(f_w13[gch % 2][ib % 4], 32 * (gch // 2 + 1))
                        if gib >= 2:
                            t_.wait_ge(f_sil, gib - 1)
                        for k in range(DC):
                            ins = t_.matmul(
                                pa[ib % 2][:],
                                wb1[gch % 2][:, (ib % 4) * 1024 + k * 128:
                                             (ib % 4) * 1024 + k * 128 + 128],
                                hbuf2[:, k * TC:(k + 1) * TC],
                                start=(k == 0), stop=(k == DC - 1))
                        ins.then_inc(f_ag, 1)
                        if gib >= 2:
                            t_.wait_ge(f_u, gib - 1)
                        for k in range(DC):
                            ins = t_.matmul(
                                pg[ib % 2][:],
                                wb3[gch % 2][:, (ib % 4) * 1024 + k * 128:
                                             (ib % 4) * 1024 + k * 128 + 128],
                                hbuf2[:, k * TC:(k + 1) * TC],
                                start=(k == 0), stop=(k == DC - 1))
                        ins.then_inc(f_ag, 1)
                    for h in range(DC):
                        gh = ge * 8 + h
                        if h == 0:
                            t_.wait_ge(f_u, NIB * (ge + 1))
                            t_.wait_ge(f_w2, 448 * (ge + 1))
                            t_.wait_ge(f_cwbv, ge + 1)
                        if h == 1 and ge >= 1:
                            t_.wait_ge(f_acc, 8 * ge)
                        if h >= 2:
                            t_.wait_ge(f_acc, gh - 1)
                        for ib in range(NIB):
                            ins = t_.matmul(peo[h % 2][:],
                                            w2sb[:, ib * D + h * 128: ib * D + h * 128 + 128],
                                            ubuf[:, ib * TC:(ib + 1) * TC],
                                            start=(ib == 0), stop=(ib == NIB - 1))
                        ins.then_inc(f_eo, 1)
                for jj in range(4):
                    gt = tc * 4 + jj
                    t_.wait_ge(f_acc, 64 * (tc + 1))
                    if gt >= 1:
                        t_.wait_ge(f_lnn, gt)
                        t_.wait_ge(f_lnsqm, 2 * gt)
                    for k in range(DC):
                        ins = t_.transpose(pln[:, k * 128:(k + 1) * 128],
                                           acc[:, k * TC + jj * 128: k * TC + jj * 128 + 128],
                                           idf[:])
                    ins.then_inc(f_lnt, 1)

        @block.scalar
        def _(s):
            for tc in range(NTC):
                for e in range(E):
                    ge = GE(tc, e)
                    for ib in range(NIB):
                        gib = ge * NIB + ib
                        s.wait_ge(f_ag, 2 * gib + 1)
                        if gib >= 2:
                            s.wait_ge(f_u, gib - 1)
                        s.activation(stmp[ib % 2][:], pa[ib % 2][:],
                                     AF.Silu).then_inc(f_sil, 1)
                for jj in range(4):
                    gt = tc * 4 + jj
                    s.wait_ge(f_lnt, gt + 1)
                    if gt >= 1:
                        s.wait_ge(f_lnv, 4 * gt)
                    s.activation(lnsq2[:], pln[:], AF.Copy,
                                 accum_out=lnws2[:, 0:1]).then_inc(f_lnsqm, 1)
                    s.wait_ge(f_lnsqm, 2 * gt + 1)
                    s.activation(lnsq2[:], pln[:], AF.Square,
                                 accum_out=lnws2[:, 1:2]).then_inc(f_lnsqm, 1)
                    s.wait_ge(f_lnv, 4 * (gt + 1))
                    s.activation(lnws2[:, 6:7], lnws2[:, 5:6], AF.Ln,
                                 bias=consts[:, C_EPS_LN:C_EPS_LN + 1]).then_inc(f_lnsd, 1)
                    s.wait_ge(f_lnsd, 2 * gt + 1)
                    s.activation(lnws2[:, 7:8], lnws2[:, 6:7], AF.Exp,
                                 scale=consts[:, C_NEGHALF:C_NEGHALF + 1]).then_inc(f_lnsd, 1)
                    s.wait_ge(f_lni, gt + 1)
                    if gt >= 4:
                        s.wait_ge(f_st[gt % 4], 16 * (gt // 4))
                    s.activation(outsb[gt % 4][:], pln[:], AF.Identity,
                                 scale=lnws2[:, 7:8], bias=lnws2[:, 9:10]).then_inc(f_lnn, 1)

        @block.vector
        def _(v):
            for tc in range(NTC):
                v.wait_ge(f_atc, 64 * (tc + 1))
                v.tensor_scalar(acc[:], acc[:], consts[:, C_TWO:C_TWO + 1], None,
                                ALU.mult).then_inc(f_acc2, 1)
                for e in range(E):
                    ge = GE(tc, e)
                    v.wait_ge(f_cwbm, ge + 1)
                    v.tensor_copy(cwb[:], peo[0][:]).then_inc(f_cwbv, 1)
                    for ib in range(NIB):
                        gib = ge * NIB + ib
                        if ib == 0 and ge >= 1:
                            v.wait_ge(f_eo, 8 * ge)
                        v.wait_ge(f_sil, gib + 1)
                        v.wait_ge(f_ag, 2 * gib + 2)
                        v.tensor_tensor(ubuf[:, ib * TC:(ib + 1) * TC],
                                        stmp[ib % 2][:], pg[ib % 2][:],
                                        ALU.mult).then_inc(f_u, 1)
                    for h in range(DC):
                        gh = ge * 8 + h
                        v.wait_ge(f_eo, gh + 1)
                        v.wait_ge(f_acc2, tc + 1)
                        if gh >= 1:
                            v.wait_ge(f_acc, gh)
                        v.tensor_tensor(vtmp[:], peo[h % 2][:], cwb[:],
                                        ALU.mult).then_inc(f_vx, 1)
                        v.wait_ge(f_vx, gh + 1)
                        v.tensor_tensor(acc[:, h * TC:(h + 1) * TC],
                                        acc[:, h * TC:(h + 1) * TC], vtmp[:],
                                        ALU.add).then_inc(f_acc, 1)
                for jj in range(4):
                    gt = tc * 4 + jj
                    v.wait_ge(f_lnsqm, 2 * (gt + 1))
                    v.tensor_scalar(lnws2[:, 2:3], lnws2[:, 0:1],
                                    consts[:, C_INV_D:C_INV_D + 1], None,
                                    ALU.mult).then_inc(f_lnv, 1)
                    v.wait_ge(f_lnv, 4 * gt + 1)
                    v.tensor_tensor(lnws2[:, 3:4], lnws2[:, 2:3], lnws2[:, 2:3],
                                    ALU.mult).then_inc(f_lnv, 1)
                    v.tensor_scalar(lnws2[:, 4:5], lnws2[:, 1:2],
                                    consts[:, C_INV_D:C_INV_D + 1], None,
                                    ALU.mult).then_inc(f_lnv, 1)
                    v.wait_ge(f_lnv, 4 * gt + 3)
                    v.tensor_tensor(lnws2[:, 5:6], lnws2[:, 4:5], lnws2[:, 3:4],
                                    ALU.subtract).then_inc(f_lnv, 1)
                    v.wait_ge(f_lnsd, 2 * (gt + 1))
                    v.tensor_tensor(lnws2[:, 8:9], lnws2[:, 2:3], lnws2[:, 7:8],
                                    ALU.mult).then_inc(f_lnw, 1)
                    v.wait_ge(f_lnw, gt + 1)
                    v.tensor_scalar(lnws2[:, 9:10], lnws2[:, 8:9],
                                    consts[:, C_NEG1:C_NEG1 + 1], None,
                                    ALU.mult).then_inc(f_lni, 1)

    return nc




def host_consts_full():
    return host_consts()


def kernel(L0_attn_in_seq, L0_attn_t_k, L0_attn_t_v, L0_attn_t_q, L0_attn_W1,
           L0_attn_b1, L0_attn_W2, L0_attn_b2, L0_ffn_rms_weight, L0_ffn_gate_w,
           L0_ffn_w1, L0_ffn_w2, L0_ffn_w3):
    x = np.ascontiguousarray(np.asarray(L0_attn_in_seq, dtype=np.float32))
    shared = {
        "tkT": np.ascontiguousarray(np.asarray(L0_attn_t_k, np.float32).T),
        "tvT": np.ascontiguousarray(np.asarray(L0_attn_t_v, np.float32).T),
        "tqT": np.ascontiguousarray(np.asarray(L0_attn_t_q, np.float32).T),
        "w1T": np.ascontiguousarray(np.asarray(L0_attn_W1, np.float32).T),
        "w2T": np.ascontiguousarray(np.asarray(L0_attn_W2, np.float32).T),
        "b1c": np.ascontiguousarray(np.asarray(L0_attn_b1, np.float32).reshape(HC, 128).T),
        "b2c": np.ascontiguousarray(np.asarray(L0_attn_b2, np.float32).reshape(DC, 128).T),
        "gate_wt": np.ascontiguousarray(np.asarray(L0_ffn_gate_w, np.float32).T),
        "rms": np.ascontiguousarray(np.asarray(L0_ffn_rms_weight, np.float32).reshape(DC, 128).T),
        "consts": host_consts(),
        "w1t": np.ascontiguousarray(
            np.asarray(L0_ffn_w1, np.float32).transpose(0, 2, 1)
            .reshape(E, DC, 128, 7, 4, 128).transpose(0, 3, 2, 4, 1, 5)
            .reshape(E * 7, 128, DC * 512)),
        "w3t": np.ascontiguousarray(
            np.asarray(L0_ffn_w3, np.float32).transpose(0, 2, 1)
            .reshape(E, DC, 128, 7, 4, 128).transpose(0, 3, 2, 4, 1, 5)
            .reshape(E * 7, 128, DC * 512)),
        "w2t": np.ascontiguousarray(
            np.asarray(L0_ffn_w2, np.float32).transpose(0, 2, 1)
            .reshape(E, NIB, 128, D).transpose(0, 2, 1, 3)
            .reshape(E, 128, NIB * D)),
    }
    nc = build()
    in_maps = []
    for c in range(8):
        m = dict(shared)
        m["xT"] = np.ascontiguousarray(
            x[:, c * NTOK:(c + 1) * NTOK, :].transpose(0, 2, 1)
            .reshape(NSTEP, DC, 128, NTOK).transpose(0, 2, 1, 3)
            .reshape(NSTEP, 128, DC * NTOK))
        in_maps.append(m)
    res = run_bass_kernel_spmd(nc, in_maps, core_ids=list(range(8)))
    parts = [np.asarray(res.results[c]["out"]).reshape(NSTEP, NTOK, D)
             for c in range(8)]
    return np.ascontiguousarray(np.concatenate(parts, axis=1))


# version 14
# speedup vs baseline: 1.0940x; 1.0940x over previous
import sys
import contextlib
import numpy as np

sys.path.insert(0, "/opt/trn_rl_repo")
import concourse.bass as bass
import concourse.mybir as mybir
from concourse.bass_utils import run_bass_kernel_spmd
from concourse.alu_op_type import AluOpType as ALU

F32 = mybir.dt.float32
BF16 = mybir.dt.bfloat16
AF = mybir.ActivationFunctionType

D = 1024
DC = 8
HI = 256
HC = HI // 128
NTOK = 256
NSTEP = 8
E = 8
LR = 1e-3
C0 = 2.0 / (2048.0 * 1024.0)
NEG_LRC0 = -LR * C0

C_EPS_RMS, C_INV_D, C_NEG_BIG, C_ZERO, C_EPS_LN, C_ONE, C_NEG1, C_NLRC0, C_TWO, C_NEGHALF = range(10)
NCI = 10
NGV = 11
T_RUN = NSTEP
CC_STUB = False
PHASE = 7
I_FF = 3584
NIB = I_FF // 128
TC = 512
NTC = (NSTEP * NTOK) // TC
NTILE = NSTEP * NTOK // 128


def host_consts():
    c = np.zeros((128, NCI), np.float32)
    c[:, C_EPS_RMS] = 1e-6
    c[:, C_INV_D] = 1.0 / D
    c[:, C_NEG_BIG] = -1e30
    c[:, C_EPS_LN] = 1e-5
    c[:, C_ONE] = 1.0
    c[:, C_NEG1] = -1.0
    c[:, C_NLRC0] = NEG_LRC0
    c[:, C_TWO] = 2.0
    c[:, C_NEGHALF] = -0.5
    return c


def build(n_cores=8):
    nc = bass.Bass("TRN2", target_bir_lowering=False)
    d_x = nc.dram_tensor("xT", [NSTEP, 128, DC * NTOK], F32, kind="ExternalInput").ap()
    d_tk = nc.dram_tensor("tkT", [D, D], F32, kind="ExternalInput").ap()
    d_tv = nc.dram_tensor("tvT", [D, D], F32, kind="ExternalInput").ap()
    d_tq = nc.dram_tensor("tqT", [D, D], F32, kind="ExternalInput").ap()
    d_w1 = nc.dram_tensor("w1T", [D, HI], F32, kind="ExternalInput").ap()
    d_w2 = nc.dram_tensor("w2T", [HI, D], F32, kind="ExternalInput").ap()
    d_b1 = nc.dram_tensor("b1c", [128, HC], F32, kind="ExternalInput").ap()
    d_b2 = nc.dram_tensor("b2c", [128, DC], F32, kind="ExternalInput").ap()
    d_gate = nc.dram_tensor("gate_wt", [D, E], F32, kind="ExternalInput").ap()
    d_rms = nc.dram_tensor("rms", [128, DC], F32, kind="ExternalInput").ap()
    d_consts = nc.dram_tensor("consts", [128, NCI], F32, kind="ExternalInput").ap()
    d_xnsp = nc.dram_tensor("xnsp", [NTILE, 128, DC * 128], F32,
                            kind="Internal").ap()
    d_hsp = nc.dram_tensor("hsp", [NTILE, 128, DC * 128], BF16,
                           kind="Internal").ap()
    d_w1t = nc.dram_tensor("w1t", [E * 7, 128, DC * 512], F32, kind="ExternalInput").ap()
    d_w3t = nc.dram_tensor("w3t", [E * 7, 128, DC * 512], F32, kind="ExternalInput").ap()
    d_w2t = nc.dram_tensor("w2t", [E, 128, NIB * D], F32, kind="ExternalInput").ap()
    d_w2p = nc.dram_tensor("w2pool", [E, 128, NIB * D], BF16, kind="Internal").ap()
    d_out = nc.dram_tensor("out", [NSTEP * NTOK, D], F32, kind="ExternalOutput").ap()
    cc_in = nc.dram_tensor("cc_in", [128, 4106], BF16, kind="Internal")
    cc_out = nc.dram_tensor("cc_out", [128, 4106], BF16, kind="Internal")

    ident_f = nc.inline_tensor(np.eye(128, dtype=np.float32), name="identf")
    ones_f = nc.inline_tensor(np.ones((1, 128), dtype=np.float32), name="onesf")

    ctx = contextlib.ExitStack()
    with ctx:
        sb = lambda name, shape, dt: ctx.enter_context(nc.sbuf_tensor(name, shape, dt))
        consts = sb("consts_sb", [128, NCI], F32)
        idf = sb("idf", [128, 128], F32)
        idb = sb("idb", [128, 128], BF16)
        rms_sb = sb("rmsw_sb", [128, DC], F32)
        ones_sb = sb("ones_sb", [1, 128], F32)
        cwT = sb("cwT", [8, NSTEP * NTOK], F32)
        mctx = contextlib.ExitStack()
        sbm = lambda name, shape, dt: mctx.enter_context(nc.sbuf_tensor(name, shape, dt))
        gate_rms = sbm("gate_rms", [128, DC * E], F32)
        tk_bf = sbm("tk_bf", [128, DC * D], BF16)
        tv_bf = sbm("tv_bf", [128, DC * D], BF16)
        tq_bf = sbm("tq_bf", [128, DC * D], BF16)
        w1t_f = sbm("w1t_f", [128, DC * HI], F32)
        w1t_bf = sbm("w1t_bf", [128, DC * HI], BF16)
        w2t_f = sbm("w2t_f", [128, HC * D], F32)
        w2t_bf = sbm("w2t_bf", [128, HC * D], BF16)
        w2n_bf = sbm("w2n_bf", [128, DC * HI], BF16)
        b1c = sbm("b1c_sb", [128, HC], F32)
        b2c = sbm("b2c_sb", [128, DC], F32)
        xbuf = sbm("xbuf", [128, DC * NTOK], F32)
        xbf = [sbm(f"xbf{p}", [128, DC * NTOK], BF16) for p in range(2)]
        zbuf = [sbm(f"zbuf{p}", [128, DC * NTOK], F32) for p in range(2)]
        zbf = [sbm(f"zbf{p}", [128, DC * NTOK], BF16) for p in range(2)]
        wstg = [sbm(f"wstg{p}", [128, 2048], BF16) for p in range(2)]
        qbf = sbm("qbf", [128, DC * NTOK], BF16)
        xq = sbm("xq", [128, DC * NTOK], F32)
        ebf = sbm("ebf", [128, DC * NTOK], BF16)
        hbf = sbm("hbf", [128, HC * NTOK], BF16)
        dgel = sbm("dgel", [128, HC * NTOK], F32)
        dla_bf = sbm("dla_bf", [128, HC * NTOK], BF16)
        ztok = sbm("ztok", [128, 2 * D], BF16)
        etok = sbm("etok", [128, 2 * D], BF16)
        htok = sbm("htok", [128, 2 * HI], BF16)
        dltok = sbm("dltok", [128, 2 * HI], BF16)
        h2bf = sbm("h2bf", [128, HC * NTOK], BF16)
        sT = sbm("sT", [128, DC * NTOK], F32)
        gbuf = sbm("gbuf", [128, 4106], BF16)
        xn_tok = sbm("xn_tok", [128, D], F32)
        hpre_tok = sbm("hpre_tok", [128, D], BF16)
        xnstage = sbm("xnstage", [128, DC * NTOK], F32)
        hstage = sbm("hstage", [128, DC * NTOK], BF16)
        lnws = sbm("lnws", [128, 16], F32)
        lnscr = sbm("lnscr", [128, 512], F32)
        lg = sbm("lg", [128, E], F32)
        gm1 = sbm("gm1", [128, E], F32)
        gm2 = sbm("gm2", [128, E], F32)
        gm3 = sbm("gm3", [128, E], F32)
        gws = sbm("gws", [128, 8], F32)
        cwtile = sbm("cwtile", [128, E], F32)

        sem = lambda name: ctx.enter_context(nc.semaphore(name))
        s_init = sem("s_init")
        s_initg = sem("s_initg")
        s_wc = [sem("s_wc0"), sem("s_wc1")]
        s_ws = [sem("s_ws0"), sem("s_ws1")]
        s_xt = sem("s_xt")
        s_xb = [sem("s_xb0"), sem("s_xb1")]
        s_prm = sem("s_prm")
        s_zm = sem("s_zm")
        s_zev = sem("s_zev")
        s_ym = sem("s_ym")
        s_yev = sem("s_yev")
        s_am = sem("s_am")
        s_hev = sem("s_hev")
        s_rm = sem("s_rm")
        s_eev = sem("s_eev")
        s_t1 = sem("s_t1")
        s_tv = sem("s_tv")
        s_dhm = sem("s_dhm")
        s_dhv = sem("s_dhv")
        s_dw2 = sem("s_dw2")
        s_gv2 = sem("s_gv2")
        s_dw1 = sem("s_dw1")
        s_qm = sem("s_qm")
        s_qev = sem("s_qev")
        s_gout = sem("s_gout")
        s_cc = sem("s_cc")
        s_gin = sem("s_gin")
        s_upd = sem("s_upd")
        s_ubf = sem("s_ubf")
        s_w2n = sem("s_w2n")
        s_w2nv = sem("s_w2nv")
        s_a2 = sem("s_a2")
        s_h2 = sem("s_h2")
        s_r2 = sem("s_r2")
        s_sT = sem("s_sT")
        s_lt = sem("s_lt")
        s_lacc = sem("s_lacc")
        s_lvv = sem("s_lvv")
        s_lvr = sem("s_lvr")
        s_le = sem("s_le")
        s_lsd = sem("s_lsd")
        s_li = sem("s_li")
        s_xnt = sem("s_xnt")
        s_rsd = sem("s_rsd")
        s_rin = sem("s_rin")
        s_hpt = sem("s_hpt")
        s_tb = sem("s_tb")
        s_stg = sem("s_stg")
        s_sp = [sem("s_sp0"), sem("s_sp1")]
        s_lgm = sem("s_lgm")
        s_gv = sem("s_gv")
        s_gs = sem("s_gs")
        s_cwt = sem("s_cwt")
        s_cwT = sem("s_cwT")
        s_cwsp = sem("s_cwsp")
        f_w13 = [[sem(f"f_w13_{p}_{q}") for q in range(4)] for p in range(2)]
        f_w2 = sem("f_w2")
        f_htc = sem("f_htc")
        f_atc = sem("f_atc")
        f_ag = sem("f_ag")
        f_sil = sem("f_sil")
        f_u = sem("f_u")
        f_eo = sem("f_eo")
        f_vx = sem("f_vx")
        f_acc = sem("f_acc")
        f_cwbm = sem("f_cwbm")
        f_cwbv = sem("f_cwbv")
        f_crow = sem("f_crow")
        f_lnt = sem("f_lnt")
        f_lnsqm = sem("f_lnsqm")
        f_lnv = sem("f_lnv")
        f_lnsd = sem("f_lnsd")
        f_lnw = sem("f_lnw")
        f_lni = sem("f_lni")
        f_lnn = sem("f_lnn")
        f_st = [sem(f"f_st{i}") for i in range(4)]
        f_acc2 = sem("f_acc2")

        block = ctx.enter_context(nc.Block())

        PBIG = mctx.enter_context(nc.psum_tensor("PBIG", [128, 2048], F32))
        PF4 = mctx.enter_context(nc.psum_tensor("PF4", [128, 512], F32))
        PF5 = mctx.enter_context(nc.psum_tensor("PF5", [128, 512], F32))
        PT = [mctx.enter_context(nc.psum_tensor(f"PT{i}", [128, 1024], BF16))
              for i in range(2)]

        def proj_tile(m):
            return PBIG[:, m * NTOK:(m + 1) * NTOK]

        @block.sync
        def _(s):
            s.dma_start(out=consts[:], in_=d_consts).then_inc(s_init, 16)
            s.dma_start(out=idf[:], in_=ident_f.ap()).then_inc(s_init, 16)
            s.dma_start(out=rms_sb[:], in_=d_rms).then_inc(s_init, 16)
            s.dma_start(out=b1c[:], in_=d_b1).then_inc(s_init, 16)
            s.dma_start(out=b2c[:], in_=d_b2).then_inc(s_init, 16)
            s.dma_start(out=gate_rms[:].rearrange("p (c e) -> p c e", c=DC),
                        in_=d_gate.rearrange("(c p) e -> p c e", p=128, c=DC)
                        ).then_inc(s_init, 16)
            s.dma_start(out=w1t_f[:].rearrange("p (c i) -> p c i", c=DC),
                        in_=d_w1.rearrange("(c p) i -> p c i", p=128, c=DC)
                        ).then_inc(s_init, 16)
            s.dma_start(out=w2t_f[:].rearrange("p (c h) -> p c h", c=HC),
                        in_=d_w2.rearrange("(c p) h -> p c h", p=128, c=HC)
                        ).then_inc(s_init, 16)
            s.dma_start(out=ones_sb[:], in_=ones_f.ap()).then_inc(s_init, 16)

            def xload(t):
                if t >= 1:
                    s.wait_ge(s_qev, 2 * t)
                s.dma_start(out=xbuf[:].rearrange("p (c n) -> p c n", c=DC),
                            in_=d_x[t].rearrange("p (c n) -> p c n", c=DC)
                            ).then_inc(s_xt, 16)

            xload(0)
            for t in range(T_RUN):
                if PHASE < 5:
                    continue
                s.wait_ge(s_gv2, 5 * (t + 1))
                s.dma_start(out=cc_in.ap(), in_=gbuf[:]).then_inc(s_gout, 16)
                s.wait_ge(s_cc, t + 1)
                s.dma_start(out=gbuf[:], in_=cc_out.ap()).then_inc(s_gin, 16)
                if t + 1 < T_RUN:
                    xload(t + 1)
                for jj in range(2):
                    gt = 2 * t + jj
                    s.wait_ge(s_stg, 2 * gt + 1)
                    s.dma_start(
                        out=d_xnsp[gt].rearrange("p (c n) -> p c n", c=DC),
                        in_=xnstage[:].rearrange("p (c n) -> p c n", c=DC)
                        [:, :, jj * 128:(jj + 1) * 128],
                    ).then_inc(s_sp[gt % 2], 16)
                    s.wait_ge(s_stg, 2 * gt + 2)
                    s.dma_start(
                        out=d_hsp[gt].rearrange("p (c n) -> p c n", c=DC),
                        in_=hstage[:].rearrange("p (c n) -> p c n", c=DC)
                        [:, :, jj * 128:(jj + 1) * 128],
                    ).then_inc(s_sp[gt % 2], 16)
            pass

        @block.gpsimd
        def _(g):
            g.dma_start(out=idb[:], in_=ident_f.ap()).then_inc(s_initg, 16)
            g.dma_start(out=tk_bf[:].rearrange("p (c h) -> p c h", c=DC),
                        in_=d_tk.rearrange("(c p) h -> p c h", p=128, c=DC)
                        ).then_inc(s_initg, 16)
            g.dma_start(out=tv_bf[:].rearrange("p (c h) -> p c h", c=DC),
                        in_=d_tv.rearrange("(c p) h -> p c h", p=128, c=DC)
                        ).then_inc(s_initg, 16)
            g.dma_start(out=tq_bf[:].rearrange("p (c h) -> p c h", c=DC),
                        in_=d_tq.rearrange("(c p) h -> p c h", p=128, c=DC)
                        ).then_inc(s_initg, 16)
            def xbload(t):
                if t >= 2:
                    g.wait_ge(s_qm, t - 1)
                g.dma_start(out=xbf[t % 2][:].rearrange("p (c n) -> p c n", c=DC),
                            in_=d_x[t].rearrange("p (c n) -> p c n", c=DC)
                            ).then_inc(s_xb[t % 2], 16)

            xbload(0)
            if T_RUN > 1:
                xbload(1)
            for t in range(T_RUN):
                if t + 2 < T_RUN:
                    xbload(t + 2)
                if PHASE < 5:
                    continue
                g.wait_ge(s_gout, 16 * (t + 1))
                g.collective_compute(
                    "AllReduce", ALU.add, [list(range(n_cores))],
                    ins=[cc_in.ap().opt()], outs=[cc_out.ap().opt()],
                ).then_inc(s_cc)
                g.wait_ge(s_cc, t + 1)
                for j in range(14):
                    w = 14 * t + j
                    if w >= 2:
                        g.wait_ge(s_ws[w % 2], 16 * (w // 2))
                    g.dma_start(out=wstg[w % 2][:],
                                in_=d_w2t[t][:, j * 2048:(j + 1) * 2048]
                                ).then_inc(s_wc[w % 2], 16)
                    g.wait_ge(s_wc[w % 2], 16 * (w // 2 + 1))
                    g.dma_start(out=d_w2p[t][:, j * 2048:(j + 1) * 2048],
                                in_=wstg[w % 2][:]).then_inc(s_ws[w % 2], 16)

        def emit_proj(t_, wsb, t):
            for m in range(DC):
                for k in range(DC):
                    ins = t_.matmul(proj_tile(m),
                                    wsb[:, k * D + m * 128: k * D + m * 128 + 128],
                                    xbf[t % 2][:, k * NTOK:(k + 1) * NTOK],
                                    start=(k == 0), stop=(k == DC - 1))
            return ins

        def emit_w2n_transposes(t_, step):
            for half in range(2):
                for k in range(4 * half, 4 * half + 4):
                    for c in range(HC):
                        idx = (k % 4) * 2 + c
                        ins = t_.transpose(
                            PT[half][:, idx * 128:idx * 128 + 128],
                            w2t_bf[:, c * D + k * 128: c * D + k * 128 + 128], idb[:])
                ins.then_inc(s_w2n, 1)

        def scalar_zevac(s, t):
            s.wait_ge(s_zev, 2 * t + 1)
            s.activation(zbf[t % 2][:], zbuf[t % 2][:], AF.Copy).then_inc(s_zev, 1)

        def vector_zevac(v, t):
            v.wait_ge(s_zm, t + 1)
            if t >= 1:
                v.wait_ge(s_zev, 2 * t)
            if t >= 2:
                v.wait_ge(s_eev, t - 1)
                v.wait_ge(s_t1, 6 * (t - 2) + 2)
            v.tensor_copy(zbuf[t % 2][:], PBIG[:]).then_inc(s_zev, 1)

        @block.tensor
        def _(t_):
            t_.wait_ge(s_init, 9 * 16)
            t_.wait_ge(s_initg, 4 * 16)
            t_.wait_ge(s_prm, 2)
            emit_w2n_transposes(t_, -1)
            t_.wait_ge(s_xb[0], 16)
            emit_proj(t_, tk_bf, 0).then_inc(s_zm, 1)
            for t in range(T_RUN):
                if PHASE < 2:
                    continue
                t_.wait_ge(s_zev, 1 if t == 0 else min(2 * (t + 1) + 1, 2 * T_RUN))
                if t >= 1:
                    t_.wait_ge(s_gv, NGV * 2 * t)
                    t_.wait_ge(s_cwT, 2 * t)
                emit_proj(t_, tv_bf, t).then_inc(s_ym, 1)
                if t >= 1:
                    t_.wait_ge(s_lacc, 8 * t)
                    t_.wait_ge(s_xnt, 2 * t)
                t_.wait_ge(s_zev, 2 * (t + 1))
                for m in range(HC):
                    for k in range(DC):
                        ins = t_.matmul(PF4[:, m * NTOK:(m + 1) * NTOK],
                                        w1t_bf[:, k * HI + m * 128: k * HI + m * 128 + 128],
                                        zbf[t % 2][:, k * NTOK:(k + 1) * NTOK],
                                        start=(k == 0), stop=(k == DC - 1))
                ins.then_inc(s_am, 1)
                if PHASE < 3:
                    continue
                t_.wait_ge(s_yev, t + 1)
                t_.wait_ge(s_hev, 2 * (t + 1))
                for m in range(DC):
                    for k in range(HC):
                        ins = t_.matmul(proj_tile(m),
                                        w2t_bf[:, k * D + m * 128: k * D + m * 128 + 128],
                                        hbf[:, k * NTOK:(k + 1) * NTOK],
                                        start=(k == 0), stop=(k == HC - 1))
                ins.then_inc(s_rm, 1)
                t_.wait_ge(s_zev, 2 * (t + 1))
                t_.wait_ge(s_w2nv, 2 * (t + 1))
                if t >= 1:
                    t_.wait_ge(s_tb, 4 * t)
                    t_.wait_ge(s_stg, 4 * t)
                for half in range(2):
                    for idx in range(8):
                        gidx = idx + half * 8
                        tc4, k = gidx // 8, gidx % 8
                        ins = t_.transpose(PT[half][:, idx * 128:(idx + 1) * 128],
                                           zbf[t % 2][:, k * NTOK + tc4 * 128:
                                                      k * NTOK + tc4 * 128 + 128], idb[:])
                    ins.then_inc(s_t1, 1)
                t_.wait_ge(s_eev, t + 1)
                t_.wait_ge(s_tv, 6 * t + 2)
                for half in range(2):
                    for idx in range(8):
                        gidx = idx + half * 8
                        tc4, k = gidx // 8, gidx % 8
                        ins = t_.transpose(PT[half][:, idx * 128:(idx + 1) * 128],
                                           ebf[:, k * NTOK + tc4 * 128:
                                               k * NTOK + tc4 * 128 + 128], idb[:])
                    ins.then_inc(s_t1, 1)
                t_.wait_ge(s_tv, 6 * t + 3)
                for idx in range(4):
                    tc4, k = idx // HC, idx % HC
                    ins = t_.transpose(PT[0][:, idx * 128:(idx + 1) * 128],
                                       hbf[:, k * NTOK + tc4 * 128:
                                           k * NTOK + tc4 * 128 + 128], idb[:])
                ins.then_inc(s_t1, 1)
                if PHASE < 4:
                    continue
                for m in range(HC):
                    for k in range(DC):
                        ins = t_.matmul(PF4[:, m * NTOK:(m + 1) * NTOK],
                                        w2n_bf[:, k * HI + m * 128: k * HI + m * 128 + 128],
                                        ebf[:, k * NTOK:(k + 1) * NTOK],
                                        start=(k == 0), stop=(k == DC - 1))
                ins.then_inc(s_dhm, 1)
                t_.wait_ge(s_dhv, 2 * t + 1)
                t_.wait_ge(s_tv, 6 * t + 4)
                for idx in range(4):
                    tc4, k = idx // HC, idx % HC
                    ins = t_.transpose(PT[1][:, idx * 128:(idx + 1) * 128],
                                       dla_bf[:, k * NTOK + tc4 * 128:
                                              k * NTOK + tc4 * 128 + 128], idb[:])
                ins.then_inc(s_t1, 1)
                t_.wait_ge(s_tv, 6 * t + 5)
                for mi in range(HC):
                    for nh in range(2):
                        q4 = mi * 2 + nh
                        if q4 > 0:
                            t_.wait_ge(s_gv2, 5 * t + q4)
                        elif t >= 1:
                            t_.wait_ge(s_lacc, 8 * t)
                            t_.wait_ge(s_xnt, 2 * t)
                        for k in range(2):
                            ins = t_.matmul(PF5[:],
                                            htok[:, k * HI + mi * 128: k * HI + mi * 128 + 128],
                                            etok[:, k * D + nh * 512: k * D + nh * 512 + 512],
                                            start=(k == 0), stop=(k == 1))
                        ins.then_inc(s_dw2, 1)
                t_.wait_ge(s_eev, t + 1)
                t_.wait_ge(s_tv, 6 * t + 6)
                for m in range(DC):
                    for k in range(2):
                        ins = t_.matmul(proj_tile(m),
                                        ztok[:, k * D + m * 128: k * D + m * 128 + 128],
                                        dltok[:, k * HI: k * HI + NTOK],
                                        start=(k == 0), stop=(k == 1))
                ins.then_inc(s_dw1, 1)
                if PHASE < 5:
                    continue
                t_.wait_ge(s_gv2, 5 * (t + 1))
                emit_proj(t_, tq_bf, t).then_inc(s_qm, 1)
                if t == 0:
                    t_.wait_ge(s_qev, 2)
                    t_.wait_ge(s_xb[1], 16)
                    emit_proj(t_, tk_bf, 1).then_inc(s_zm, 1)
                if t + 2 < T_RUN:
                    if t == 0:
                        t_.wait_ge(s_zev, 3)
                    else:
                        t_.wait_ge(s_qev, 2 * (t + 1))
                    t_.wait_ge(s_xb[t % 2], 16 * ((t + 2) // 2 + 1))
                    emit_proj(t_, tk_bf, t + 2).then_inc(s_zm, 1)
                t_.wait_ge(s_ubf, 2 * t + 1)
                if PHASE < 6:
                    continue
                t_.wait_ge(s_dhv, 2 * t + 1)
                for m in range(HC):
                    for k in range(DC):
                        ins = t_.matmul(PF4[:, m * NTOK:(m + 1) * NTOK],
                                        w1t_bf[:, k * HI + m * 128: k * HI + m * 128 + 128],
                                        qbf[:, k * NTOK:(k + 1) * NTOK],
                                        start=(k == 0), stop=(k == DC - 1))
                ins.then_inc(s_a2, 1)
                t_.wait_ge(s_ubf, 2 * (t + 1))
                t_.wait_ge(s_tv, 6 * (t + 1))
                emit_w2n_transposes(t_, t)
                if t + 2 < T_RUN:
                    t_.wait_ge(s_zev, 2 * (t + 2) + 1)
                else:
                    t_.wait_ge(s_qev, 2 * (t + 1))
                t_.wait_ge(s_h2, t + 1)
                for m in range(DC):
                    for k in range(HC):
                        ins = t_.matmul(proj_tile(m),
                                        w2t_bf[:, k * D + m * 128: k * D + m * 128 + 128],
                                        h2bf[:, k * NTOK:(k + 1) * NTOK],
                                        start=(k == 0), stop=(k == HC - 1))
                ins.then_inc(s_r2, 1)
                if PHASE < 7:
                    continue
                for jj in range(2):
                    gt = 2 * t + jj
                    t_.wait_ge(s_sT, t + 1)
                    if jj == 0:
                        t_.wait_ge(s_h2, t + 1)
                    else:
                        t_.wait_ge(s_lacc, 4 * gt)
                        t_.wait_ge(s_xnt, gt)
                    for k in range(DC):
                        dst = (PF4 if k < 4 else PF5)[:, (k % 4) * 128:(k % 4) * 128 + 128]
                        ins = t_.transpose(dst, sT[:, k * NTOK + jj * 128:
                                                   k * NTOK + jj * 128 + 128], idf[:])
                    ins.then_inc(s_lt, 1)
                    t_.wait_ge(s_xnt, gt + 1)
                    if gt >= 1:
                        t_.wait_ge(s_stg, 2 * gt - 1)
                    for k in range(DC):
                        dst = PBIG[:, k * 128:(k + 1) * 128]
                        ins = t_.transpose(dst, xn_tok[:, k * 128:(k + 1) * 128], idf[:])
                    ins.then_inc(s_tb, 1)
                    t_.wait_ge(s_hpt, gt + 1)
                    t_.wait_ge(s_w2nv, 2 * (t + 2))
                    if gt >= 1:
                        t_.wait_ge(s_stg, 2 * gt)
                    for k in range(DC):
                        ins = t_.transpose(PT[0][:, k * 128:(k + 1) * 128],
                                           hpre_tok[:, k * 128:(k + 1) * 128], idb[:])
                    ins.then_inc(s_tb, 1)
                    t_.wait_ge(s_stg, 2 * gt + 1)
                    t_.wait_ge(s_gs, 1)
                    if gt >= 1:
                        t_.wait_ge(s_gv, NGV * (gt - 1) + 1)
                    for k in range(DC):
                        ins = t_.matmul(PBIG[:, 1024:1032],
                                        xnstage[:, k * NTOK + jj * 128:
                                                k * NTOK + jj * 128 + 128],
                                        gate_rms[:, k * E:(k + 1) * E],
                                        start=(k == 0), stop=(k == DC - 1))
                    ins.then_inc(s_lgm, 1)
                    t_.wait_ge(s_gv, NGV * (gt + 1))
                    if gt >= 1:
                        t_.wait_ge(s_cwT, gt)
                    t_.transpose(PBIG[0:8, 1536:1664], cwtile[:], idf[:]).then_inc(s_cwt, 1)

        @block.scalar
        def _(s):
            s.wait_ge(s_init, 9 * 16)
            for k in range(DC):
                ins = s.activation(gate_rms[:, k * E:(k + 1) * E],
                                   gate_rms[:, k * E:(k + 1) * E],
                                   AF.Copy, scale=rms_sb[:, k:k + 1])
            ins.then_inc(s_gs, 1)
            s.activation(w1t_bf[:], w1t_f[:], AF.Copy).then_inc(s_prm, 1)
            s.activation(w2t_bf[:], w2t_f[:], AF.Copy).then_inc(s_prm, 1)
            scalar_zevac(s, 0)
            for t in range(T_RUN):
                if PHASE < 2:
                    continue
                s.wait_ge(s_am, t + 1)
                if t >= 1:
                    s.wait_ge(s_dhv, 2 * (t - 1) + 1)
                    s.wait_ge(s_t1, 6 * (t - 1) + 5)
                for m in range(HC):
                    s.activation(hbf[:, m * NTOK:(m + 1) * NTOK],
                                 PF4[:, m * NTOK:(m + 1) * NTOK], AF.Gelu,
                                 bias=b1c[:, m:m + 1])
                for m in range(HC):
                    ins = s.activation(dgel[:, m * NTOK:(m + 1) * NTOK],
                                       PF4[:, m * NTOK:(m + 1) * NTOK],
                                       AF.Derivative_Gelu, bias=b1c[:, m:m + 1])
                ins.then_inc(s_hev, 2)
                if PHASE < 3:
                    continue

                if PHASE < 4:
                    continue

                if PHASE < 5:
                    continue
                s.wait_ge(s_qm, t + 1)
                s.wait_ge(s_qev, 2 * t + 1)
                if t >= 1:
                    s.wait_ge(s_a2, t)
                s.activation(qbf[:], PBIG[:], AF.Copy).then_inc(s_qev, 1)
                if t == 0:
                    scalar_zevac(s, 1)
                if t + 2 < T_RUN:
                    scalar_zevac(s, t + 2)
                s.wait_ge(s_upd, 4 * t + 1)
                s.activation(w1t_bf[:], w1t_f[:], AF.Copy).then_inc(s_ubf, 1)
                s.wait_ge(s_upd, 4 * t + 2)
                s.activation(w2t_bf[:], w2t_f[:], AF.Copy).then_inc(s_ubf, 1)
                if PHASE < 6:
                    continue
                s.wait_ge(s_upd, 4 * t + 3)
                s.wait_ge(s_a2, t + 1)
                for m in range(HC):
                    ins = s.activation(h2bf[:, m * NTOK:(m + 1) * NTOK],
                                       PF4[:, m * NTOK:(m + 1) * NTOK], AF.Gelu,
                                       bias=b1c[:, m:m + 1])
                ins.then_inc(s_h2, 1)
                if PHASE < 7:
                    continue
                for jj in range(2):
                    gt = 2 * t + jj
                    s.wait_ge(s_lt, gt + 1)
                    if gt >= 1:
                        s.wait_ge(s_lvv, gt)
                    s.activation(lnscr[:], PF4[:], AF.Copy,
                                 accum_out=lnws[:, 0:1]).then_inc(s_lacc, 1)
                    s.wait_ge(s_lacc, 4 * gt + 1)
                    s.activation(lnscr[:], PF5[:], AF.Copy,
                                 accum_out=lnws[:, 1:2]).then_inc(s_lacc, 1)
                    s.wait_ge(s_lacc, 4 * gt + 2)
                    s.activation(lnscr[:], PF4[:], AF.Square,
                                 accum_out=lnws[:, 2:3]).then_inc(s_lacc, 1)
                    s.wait_ge(s_lacc, 4 * gt + 3)
                    s.activation(lnscr[:], PF5[:], AF.Square,
                                 accum_out=lnws[:, 3:4]).then_inc(s_lacc, 1)
                    s.wait_ge(s_lvv, gt + 1)
                    s.activation(lnws[:, 8:9], lnws[:, 7:8], AF.Ln,
                                 bias=consts[:, C_EPS_LN:C_EPS_LN + 1]).then_inc(s_lsd, 1)
                    s.wait_ge(s_lsd, 2 * gt + 1)
                    s.activation(lnws[:, 9:10], lnws[:, 8:9], AF.Exp,
                                 scale=consts[:, C_NEGHALF:C_NEGHALF + 1]).then_inc(s_lsd, 1)
                    s.wait_ge(s_li, gt + 1)
                    if gt >= 1:
                        s.wait_ge(s_tb, 2 * gt - 1)
                        s.wait_ge(s_hpt, gt)
                    s.activation(xn_tok[:, 0:512], PF4[:], AF.Identity,
                                 scale=lnws[:, 9:10], bias=lnws[:, 10:11])
                    ins = s.activation(xn_tok[:, 512:1024], PF5[:], AF.Identity,
                                       scale=lnws[:, 9:10], bias=lnws[:, 10:11])
                    ins.then_inc(s_xnt, 1)
                    s.wait_ge(s_lvr, gt + 1)
                    s.activation(lnws[:, 14:15], lnws[:, 13:14], AF.Ln,
                                 bias=consts[:, C_EPS_RMS:C_EPS_RMS + 1]).then_inc(s_rsd, 1)
                    s.wait_ge(s_rsd, 2 * gt + 1)
                    s.activation(lnws[:, 15:16], lnws[:, 14:15], AF.Exp,
                                 scale=consts[:, C_NEGHALF:C_NEGHALF + 1]).then_inc(s_rsd, 1)
                    s.wait_ge(s_rsd, 2 * (gt + 1))
                    if gt >= 1:
                        s.wait_ge(s_tb, 2 * gt)
                    s.activation(hpre_tok[:], xn_tok[:], AF.Copy,
                                 scale=lnws[:, 15:16]).then_inc(s_hpt, 1)
                    s.wait_ge(s_gv, NGV * gt + 2)
                    s.activation(gm3[:], lg[:], AF.Exp, bias=gws[:, 0:1]).then_inc(s_gs, 1)
                    s.wait_ge(s_gv, NGV * gt + 7)
                    s.activation(gws[:, 3:4], gws[:, 2:3], AF.Exp).then_inc(s_gs, 1)

        @block.vector
        def _(v):
            self_edge = [0]
            for half in range(2):
                v.wait_ge(s_w2n, half + 1)
                v.tensor_copy(w2n_bf[:, half * 1024:(half + 1) * 1024],
                              PT[half][:]).then_inc(s_w2nv, 1)
            vector_zevac(v, 0)
            for t in range(T_RUN):
                if PHASE < 2:
                    continue
                v.wait_ge(s_ym, t + 1)
                v.wait_ge(s_zev, 2 * (t + 1))
                for m in range(DC):
                    ins = v.scalar_tensor_tensor(zbuf[t % 2][:, m * NTOK:(m + 1) * NTOK],
                                                 zbuf[t % 2][:, m * NTOK:(m + 1) * NTOK],
                                                 b2c[:, m:m + 1], proj_tile(m),
                                                 ALU.add, ALU.subtract)
                ins.then_inc(s_yev, 1)
                if PHASE < 3:
                    continue
                v.wait_ge(s_rm, t + 1)
                v.wait_ge(s_yev, t + 1)
                if t >= 1:
                    v.wait_ge(s_dhm, t)
                    v.wait_ge(s_t1, 6 * (t - 1) + 4)
                v.tensor_tensor(ebf[:], zbuf[t % 2][:], PBIG[:],
                                ALU.add).then_inc(s_eev, 1)
                v.wait_ge(s_t1, 6 * t + 1)
                v.tensor_copy(ztok[:, 0:1024], PT[0][:]).then_inc(s_tv, 1)
                v.wait_ge(s_t1, 6 * t + 2)
                v.tensor_copy(ztok[:, 1024:2048], PT[1][:]).then_inc(s_tv, 1)
                v.wait_ge(s_t1, 6 * t + 3)
                v.tensor_copy(etok[:, 0:1024], PT[0][:]).then_inc(s_tv, 1)
                v.wait_ge(s_t1, 6 * t + 4)
                v.tensor_copy(etok[:, 1024:2048], PT[1][:]).then_inc(s_tv, 1)
                v.wait_ge(s_t1, 6 * t + 5)
                v.tensor_copy(htok[:], PT[0][:, 0:512]).then_inc(s_tv, 1)
                if PHASE < 4:
                    continue
                v.wait_ge(s_dhm, t + 1)
                if t >= 1:
                    v.wait_ge(s_t1, 6 * (t - 1) + 6)
                for m in range(HC):
                    ins = v.tensor_tensor(dla_bf[:, m * NTOK:(m + 1) * NTOK],
                                          dgel[:, m * NTOK:(m + 1) * NTOK],
                                          PF4[:, m * NTOK:(m + 1) * NTOK], ALU.mult)
                ins.then_inc(s_dhv, 1)
                v.wait_ge(s_dhv, 2 * t + 1)
                v.reduce_sum(lnscr[:, 16:17], dla_bf[:, 0:NTOK],
                             axis=mybir.AxisListType.X)
                v.reduce_sum(lnscr[:, 17:18], dla_bf[:, NTOK:2 * NTOK],
                             axis=mybir.AxisListType.X).then_inc(s_le, 1)
                self_edge[0] += 1
                v.wait_ge(s_le, self_edge[0])
                ins = v.tensor_copy(gbuf[:, 4096:4098], lnscr[:, 16:18])
                ins.then_inc(s_dhv, 1)
                v.wait_ge(s_t1, 6 * t + 6)
                v.tensor_copy(dltok[:], PT[1][:, 0:512]).then_inc(s_tv, 1)
                for q4 in range(4):
                    mi, nh = q4 // 2, q4 % 2
                    v.wait_ge(s_dw2, 4 * t + q4 + 1)
                    ins = v.tensor_copy(
                        gbuf[:, 2048 + mi * 1024 + nh * 512:
                             2048 + mi * 1024 + nh * 512 + 512], PF5[:])
                    ins.then_inc(s_gv2, 1)
                for m in range(DC):
                    ins = v.reduce_sum(lnscr[:, 20 + m:21 + m],
                                       ebf[:, m * NTOK:(m + 1) * NTOK],
                                       axis=mybir.AxisListType.X)
                ins.then_inc(s_le, 1)
                self_edge[0] += 1
                v.wait_ge(s_le, self_edge[0])
                v.tensor_copy(gbuf[:, 4098:4106], lnscr[:, 20:28])
                v.wait_ge(s_dw1, t + 1)
                v.tensor_copy(gbuf[:, 0:2048], PBIG[:]).then_inc(s_gv2, 1)
                if PHASE < 5:
                    continue
                v.wait_ge(s_qm, t + 1)
                v.wait_ge(s_xt, 16 * (t + 1))
                if t >= 1:
                    v.wait_ge(s_sT, t)
                v.tensor_tensor(xq[:], PBIG[:], xbuf[:], ALU.add).then_inc(s_qev, 1)
                if t == 0:
                    vector_zevac(v, 1)
                if t + 2 < T_RUN:
                    vector_zevac(v, t + 2)
                v.wait_ge(s_gin, 16 * (t + 1))
                v.wait_ge(s_ubf, 2 * t)
                v.scalar_tensor_tensor(w1t_f[:], gbuf[:, 0:2048],
                                       consts[:, C_NLRC0:C_NLRC0 + 1], w1t_f[:],
                                       ALU.mult, ALU.add).then_inc(s_upd, 1)
                v.scalar_tensor_tensor(w2t_f[:], gbuf[:, 2048:4096],
                                       consts[:, C_NLRC0:C_NLRC0 + 1], w2t_f[:],
                                       ALU.mult, ALU.add).then_inc(s_upd, 1)
                v.scalar_tensor_tensor(b1c[:], gbuf[:, 4096:4098],
                                       consts[:, C_NLRC0:C_NLRC0 + 1], b1c[:],
                                       ALU.mult, ALU.add).then_inc(s_upd, 1)
                v.scalar_tensor_tensor(b2c[:], gbuf[:, 4098:4106],
                                       consts[:, C_NLRC0:C_NLRC0 + 1], b2c[:],
                                       ALU.mult, ALU.add).then_inc(s_upd, 1)
                for half in range(2):
                    v.wait_ge(s_w2n, 2 * (t + 1) + half + 1)
                    v.tensor_copy(w2n_bf[:, half * 1024:(half + 1) * 1024],
                                  PT[half][:]).then_inc(s_w2nv, 1)
                if PHASE < 6:
                    continue
                v.wait_ge(s_r2, t + 1)
                v.wait_ge(s_upd, 4 * (t + 1))
                v.wait_ge(s_qev, 2 * (t + 1))
                for m in range(DC):
                    ins = v.scalar_tensor_tensor(sT[:, m * NTOK:(m + 1) * NTOK],
                                                 proj_tile(m), b2c[:, m:m + 1],
                                                 xq[:, m * NTOK:(m + 1) * NTOK],
                                                 ALU.add, ALU.add)
                ins.then_inc(s_sT, 1)
                if PHASE < 7:
                    continue
                for jj in range(2):
                    gt = 2 * t + jj

                    def vtick(ins):
                        self_edge[0] += 1
                        ins.then_inc(s_le, 1)

                    def vedge():
                        v.wait_ge(s_le, self_edge[0])

                    v.wait_ge(s_lacc, 4 * (gt + 1))
                    vtick(v.tensor_tensor(lnws[:, 4:5], lnws[:, 0:1], lnws[:, 1:2],
                                          ALU.add))
                    vedge()
                    vtick(v.tensor_scalar(lnws[:, 5:6], lnws[:, 4:5],
                                          consts[:, C_INV_D:C_INV_D + 1], None,
                                          ALU.mult))
                    vtick(v.tensor_tensor(lnws[:, 6:7], lnws[:, 2:3], lnws[:, 3:4],
                                          ALU.add))
                    vedge()
                    vtick(v.tensor_scalar(lnws[:, 6:7], lnws[:, 6:7],
                                          consts[:, C_INV_D:C_INV_D + 1], None,
                                          ALU.mult))
                    vtick(v.tensor_tensor(lnscr[:, 0:1], lnws[:, 5:6], lnws[:, 5:6],
                                          ALU.mult))
                    vedge()
                    v.tensor_tensor(lnws[:, 7:8], lnws[:, 6:7], lnscr[:, 0:1],
                                    ALU.subtract).then_inc(s_lvv, 1)
                    v.wait_ge(s_lsd, 2 * (gt + 1))
                    vtick(v.tensor_tensor(lnscr[:, 1:2], lnws[:, 5:6], lnws[:, 9:10],
                                          ALU.mult))
                    vedge()
                    v.tensor_scalar(lnws[:, 10:11], lnscr[:, 1:2],
                                    consts[:, C_NEG1:C_NEG1 + 1], None,
                                    ALU.mult).then_inc(s_li, 1)
                    if gt >= 1:
                        v.wait_ge(s_rsd, 2 * gt - 1)
                    vtick(v.tensor_tensor(lnws[:, 11:12], lnws[:, 9:10], lnws[:, 9:10],
                                          ALU.mult))
                    vedge()
                    v.tensor_tensor(lnws[:, 13:14], lnws[:, 7:8], lnws[:, 11:12],
                                    ALU.mult).then_inc(s_lvr, 1)
                    v.wait_ge(s_tb, 2 * gt + 1)
                    if gt >= 1:
                        v.wait_ge(s_sp[(gt - 1) % 2], 32 * ((gt - 1) // 2 + 1))
                    if gt >= 2:
                        v.wait_ge(s_sp[gt % 2], 32 * (gt // 2))
                    dst = xnstage[:].rearrange("p (k t) -> p k t", k=DC)[
                        :, :, jj * 128:jj * 128 + 128]
                    ins = v.tensor_copy(dst, PBIG[:, 0:1024]
                                        .rearrange("p (k t) -> p k t", k=DC))
                    ins.then_inc(s_stg, 1)
                    v.wait_ge(s_tb, 2 * gt + 2)
                    for k in range(DC):
                        ins = v.tensor_scalar(
                            hstage[:, k * NTOK + jj * 128: k * NTOK + jj * 128 + 128],
                            PT[0][:, k * 128:(k + 1) * 128],
                            rms_sb[:, k:k + 1], None, ALU.mult)
                    ins.then_inc(s_stg, 1)
                    v.wait_ge(s_lgm, gt + 1)
                    v.tensor_copy(lg[:], PBIG[:, 1024:1032]).then_inc(s_gv, 1)
                    v.wait_ge(s_gv, NGV * gt + 1)
                    v.reduce_max(gws[:, 0:1], lg[:], axis=mybir.AxisListType.X,
                                 negate=True).then_inc(s_gv, 1)
                    v.wait_ge(s_gv, NGV * gt + 2)
                    v.tensor_scalar(gm1[:], lg[:], gws[:, 0:1],
                                    consts[:, C_ZERO:C_ZERO + 1],
                                    ALU.add, ALU.is_equal).then_inc(s_gv, 1)
                    v.wait_ge(s_gv, NGV * gt + 3)
                    v.scalar_tensor_tensor(gm2[:], gm1[:],
                                           consts[:, C_NEG_BIG:C_NEG_BIG + 1],
                                           lg[:], ALU.mult, ALU.add).then_inc(s_gv, 1)
                    v.wait_ge(s_gv, NGV * gt + 4)
                    v.reduce_max(gws[:, 1:2], gm2[:], axis=mybir.AxisListType.X,
                                 negate=True).then_inc(s_gv, 1)
                    v.wait_ge(s_gv, NGV * gt + 5)
                    v.tensor_scalar(gm2[:], lg[:], gws[:, 1:2],
                                    consts[:, C_ZERO:C_ZERO + 1],
                                    ALU.add, ALU.is_ge).then_inc(s_gv, 1)
                    v.tensor_tensor(gws[:, 2:3], gws[:, 0:1], gws[:, 1:2],
                                    ALU.subtract).then_inc(s_gv, 1)
                    v.wait_ge(s_gs, 2 * gt + 3)
                    v.tensor_scalar(gws[:, 4:5], gws[:, 3:4],
                                    consts[:, C_ONE:C_ONE + 1], None,
                                    ALU.add).then_inc(s_gv, 1)
                    v.wait_ge(s_gv, NGV * gt + 8)
                    v.reciprocal(gws[:, 5:6], gws[:, 4:5]).then_inc(s_gv, 1)
                    v.wait_ge(s_gs, 2 * gt + 2)
                    v.tensor_tensor(gm1[:], gm2[:], gm3[:], ALU.mult).then_inc(s_gv, 1)
                    v.wait_ge(s_gv, NGV * gt + 10)
                    if gt >= 1:
                        v.wait_ge(s_cwt, gt)
                    v.tensor_scalar(cwtile[:], gm1[:], gws[:, 5:6], None,
                                    ALU.mult).then_inc(s_gv, 1)
                    v.wait_ge(s_cwt, gt + 1)
                    v.tensor_copy(cwT[0:8, gt * 128:(gt + 1) * 128],
                                  PBIG[0:8, 1536:1664]).then_inc(s_cwT, 1)


        mctx.close()

        fsb = lambda name, shape, dt: ctx.enter_context(nc.sbuf_tensor(name, shape, dt))
        cwrow = fsb("cwrow", [1, TC], F32)
        cwb = fsb("cwb", [128, TC], F32)
        hbuf2 = fsb("hbuf2", [128, DC * TC], BF16)
        acc = fsb("acc", [128, DC * TC], F32)
        ubuf = fsb("ubuf", [128, NIB * TC], BF16)
        w2sb = fsb("w2sb", [128, NIB * D], BF16)
        wb1 = [fsb(f"wb1_{p}", [128, DC * 512], BF16) for p in range(2)]
        wb3 = [fsb(f"wb3_{p}", [128, DC * 512], BF16) for p in range(2)]
        stmp = [fsb(f"stmp_{p}", [128, TC], F32) for p in range(2)]
        vtmp = fsb("vtmp", [128, TC], F32)
        outsb = [fsb(f"outsb_{p}", [128, D], F32) for p in range(4)]
        lnsq2 = fsb("lnsq2", [128, D], F32)
        lnws2 = fsb("lnws2", [128, 10], F32)

        pa = [ctx.enter_context(nc.psum_tensor(f"pa{p}", [128, TC], F32)) for p in range(2)]
        pg = [ctx.enter_context(nc.psum_tensor(f"pg{p}", [128, TC], F32)) for p in range(2)]
        peo = [ctx.enter_context(nc.psum_tensor(f"peo{p}", [128, TC], F32)) for p in range(2)]
        pln = ctx.enter_context(nc.psum_tensor("pln", [128, D], F32))

        def GE(tc, e):
            return tc * E + e

        @block.gpsimd
        def _(g):
            g.wait_ge(s_cwT, NTILE)
            g.wait_ge(s_sp[0], 32 * NSTEP)
            g.wait_ge(s_sp[1], 32 * NSTEP)
            g.wait_ge(s_ws[0], 16 * 56)
            g.wait_ge(s_ws[1], 16 * 56)
            for tc in range(NTC):
                for e in range(E):
                    ge = GE(tc, e)
                    for ic in range(7):
                        gch = ge * 7 + ic
                        if gch >= 2:
                            g.wait_ge(f_ag, 2 * (4 * (gch - 2) + 4))
                        for q in range(4):
                            g.dma_start(out=wb1[gch % 2][:, q * 1024:(q + 1) * 1024],
                                        in_=d_w1t[e * 7 + ic][:, q * 1024:(q + 1) * 1024]
                                        ).then_inc(f_w13[gch % 2][q], 16)
                            g.dma_start(out=wb3[gch % 2][:, q * 1024:(q + 1) * 1024],
                                        in_=d_w3t[e * 7 + ic][:, q * 1024:(q + 1) * 1024]
                                        ).then_inc(f_w13[gch % 2][q], 16)
                        if 1 <= ic <= 4:
                            r0 = (ic - 1) * 7
                            if r0 == 0:
                                g.wait_ge(f_eo, 8 * ge)
                            for piece in range(r0, r0 + 7):
                                g.dma_start(out=w2sb[:, piece * D:(piece + 1) * D],
                                            in_=d_w2p[e][:, piece * D:(piece + 1) * D]
                                            ).then_inc(f_w2, 16)

        @block.sync
        def _(s):
            s.wait_ge(s_cwT, NTILE)
            s.wait_ge(s_sp[0], 32 * NSTEP)
            s.wait_ge(s_sp[1], 32 * NSTEP)

            def load_tc(tc):
                s.wait_ge(s_sp[0], 32 * 2 * (tc + 1))
                s.wait_ge(s_sp[1], 32 * 2 * (tc + 1))
                if tc >= 1:
                    s.wait_ge(f_ag, 2 * 28 * 8 * tc)
                for q in range(4):
                    s.dma_start(out=hbuf2[:].rearrange("p (c t) -> p c t", c=DC)
                                [:, :, q * 128:(q + 1) * 128],
                                in_=d_hsp[tc * 4 + q].rearrange("p (c n) -> p c n", c=DC)
                                ).then_inc(f_htc, 16)
                if tc >= 1:
                    s.wait_ge(f_lnt, 4 * tc)
                for q in range(4):
                    s.dma_start(out=acc[:].rearrange("p (c t) -> p c t", c=DC)
                                [:, :, q * 128:(q + 1) * 128],
                                in_=d_xnsp[tc * 4 + q].rearrange("p (c n) -> p c n", c=DC)
                                ).then_inc(f_atc, 16)

            def cwrows(tc):
                for e in range(E):
                    ge = GE(tc, e)
                    s.wait_ge(s_cwT, 4 * (tc + 1))
                    s.wait_ge(f_cwbm, ge)
                    s.dma_start(out=cwrow[:],
                                in_=cwT[e:e + 1, tc * TC:(tc + 1) * TC]).then_inc(f_crow, 16)

            load_tc(0)
            cwrows(0)
            for tc in range(NTC):
                if tc + 1 < NTC:
                    load_tc(tc + 1)
                    cwrows(tc + 1)
                for jj in range(4):
                    gt = tc * 4 + jj
                    s.wait_ge(f_lnn, gt + 1)
                    s.dma_start(out=d_out[gt * 128:(gt + 1) * 128, :],
                                in_=outsb[gt % 4][:]).then_inc(f_st[gt % 4], 16)
            for q in range(4):
                s.wait_ge(f_st[q], 16 * (NTILE // 4))

        @block.tensor
        def _(t_):
            t_.wait_ge(s_cwT, NTILE)
            for tc in range(NTC):
                t_.wait_ge(f_htc, 64 * (tc + 1))
                for e in range(E):
                    ge = GE(tc, e)
                    t_.wait_ge(f_crow, 16 * (ge + 1))
                    if ge >= 1:
                        t_.wait_ge(f_acc, 8 * (ge - 1) + 7)
                    t_.matmul(peo[0][:], ones_sb[:], cwrow[:],
                              start=True, stop=True).then_inc(f_cwbm, 1)
                    for ib in range(NIB):
                        gib = ge * NIB + ib
                        gch = ge * 7 + ib // 4
                        t_.wait_ge(f_w13[gch % 2][ib % 4], 32 * (gch // 2 + 1))
                        if gib >= 2:
                            t_.wait_ge(f_sil, gib - 1)
                            t_.wait_ge(f_u, gib - 1)
                        for k in range(DC):
                            ins_a = t_.matmul(
                                pa[ib % 2][:],
                                wb1[gch % 2][:, (ib % 4) * 1024 + k * 128:
                                             (ib % 4) * 1024 + k * 128 + 128],
                                hbuf2[:, k * TC:(k + 1) * TC],
                                start=(k == 0), stop=(k == DC - 1))
                            ins_g = t_.matmul(
                                pg[ib % 2][:],
                                wb3[gch % 2][:, (ib % 4) * 1024 + k * 128:
                                             (ib % 4) * 1024 + k * 128 + 128],
                                hbuf2[:, k * TC:(k + 1) * TC],
                                start=(k == 0), stop=(k == DC - 1))
                        ins_a.then_inc(f_ag, 1)
                        ins_g.then_inc(f_ag, 1)
                    for h in range(DC):
                        gh = ge * 8 + h
                        if h == 0:
                            t_.wait_ge(f_u, NIB * (ge + 1))
                            t_.wait_ge(f_w2, 448 * (ge + 1))
                            t_.wait_ge(f_cwbv, ge + 1)
                        if h == 1 and ge >= 1:
                            t_.wait_ge(f_acc, 8 * ge)
                        if h >= 2:
                            t_.wait_ge(f_acc, gh - 1)
                        for ib in range(NIB):
                            ins = t_.matmul(peo[h % 2][:],
                                            w2sb[:, ib * D + h * 128: ib * D + h * 128 + 128],
                                            ubuf[:, ib * TC:(ib + 1) * TC],
                                            start=(ib == 0), stop=(ib == NIB - 1))
                        ins.then_inc(f_eo, 1)
                for jj in range(4):
                    gt = tc * 4 + jj
                    t_.wait_ge(f_acc, 64 * (tc + 1))
                    if gt >= 1:
                        t_.wait_ge(f_lnn, gt)
                        t_.wait_ge(f_lnsqm, 2 * gt)
                    for k in range(DC):
                        ins = t_.transpose(pln[:, k * 128:(k + 1) * 128],
                                           acc[:, k * TC + jj * 128: k * TC + jj * 128 + 128],
                                           idf[:])
                    ins.then_inc(f_lnt, 1)

        @block.scalar
        def _(s):
            for tc in range(NTC):
                for e in range(E):
                    ge = GE(tc, e)
                    for ib in range(NIB):
                        gib = ge * NIB + ib
                        s.wait_ge(f_ag, 2 * gib + 1)
                        if gib >= 2:
                            s.wait_ge(f_u, gib - 1)
                        s.activation(stmp[ib % 2][:], pa[ib % 2][:],
                                     AF.Silu).then_inc(f_sil, 1)
                for jj in range(4):
                    gt = tc * 4 + jj
                    s.wait_ge(f_lnt, gt + 1)
                    if gt >= 1:
                        s.wait_ge(f_lnv, 4 * gt)
                    s.activation(lnsq2[:], pln[:], AF.Copy,
                                 accum_out=lnws2[:, 0:1]).then_inc(f_lnsqm, 1)
                    s.wait_ge(f_lnsqm, 2 * gt + 1)
                    s.activation(lnsq2[:], pln[:], AF.Square,
                                 accum_out=lnws2[:, 1:2]).then_inc(f_lnsqm, 1)
                    s.wait_ge(f_lnv, 4 * (gt + 1))
                    s.activation(lnws2[:, 6:7], lnws2[:, 5:6], AF.Ln,
                                 bias=consts[:, C_EPS_LN:C_EPS_LN + 1]).then_inc(f_lnsd, 1)
                    s.wait_ge(f_lnsd, 2 * gt + 1)
                    s.activation(lnws2[:, 7:8], lnws2[:, 6:7], AF.Exp,
                                 scale=consts[:, C_NEGHALF:C_NEGHALF + 1]).then_inc(f_lnsd, 1)
                    s.wait_ge(f_lni, gt + 1)
                    if gt >= 4:
                        s.wait_ge(f_st[gt % 4], 16 * (gt // 4))
                    s.activation(outsb[gt % 4][:], pln[:], AF.Identity,
                                 scale=lnws2[:, 7:8], bias=lnws2[:, 9:10]).then_inc(f_lnn, 1)

        @block.vector
        def _(v):
            for tc in range(NTC):
                v.wait_ge(f_atc, 64 * (tc + 1))
                v.tensor_scalar(acc[:], acc[:], consts[:, C_TWO:C_TWO + 1], None,
                                ALU.mult).then_inc(f_acc2, 1)
                for e in range(E):
                    ge = GE(tc, e)
                    v.wait_ge(f_cwbm, ge + 1)
                    v.tensor_copy(cwb[:], peo[0][:]).then_inc(f_cwbv, 1)
                    for ib in range(NIB):
                        gib = ge * NIB + ib
                        if ib == 0 and ge >= 1:
                            v.wait_ge(f_eo, 8 * ge)
                        v.wait_ge(f_sil, gib + 1)
                        v.wait_ge(f_ag, 2 * gib + 2)
                        v.tensor_tensor(ubuf[:, ib * TC:(ib + 1) * TC],
                                        stmp[ib % 2][:], pg[ib % 2][:],
                                        ALU.mult).then_inc(f_u, 1)
                    for h in range(DC):
                        gh = ge * 8 + h
                        v.wait_ge(f_eo, gh + 1)
                        v.wait_ge(f_acc2, tc + 1)
                        if gh >= 1:
                            v.wait_ge(f_acc, gh)
                        v.tensor_tensor(vtmp[:], peo[h % 2][:], cwb[:],
                                        ALU.mult).then_inc(f_vx, 1)
                        v.wait_ge(f_vx, gh + 1)
                        v.tensor_tensor(acc[:, h * TC:(h + 1) * TC],
                                        acc[:, h * TC:(h + 1) * TC], vtmp[:],
                                        ALU.add).then_inc(f_acc, 1)
                for jj in range(4):
                    gt = tc * 4 + jj
                    v.wait_ge(f_lnsqm, 2 * (gt + 1))
                    v.tensor_scalar(lnws2[:, 2:3], lnws2[:, 0:1],
                                    consts[:, C_INV_D:C_INV_D + 1], None,
                                    ALU.mult).then_inc(f_lnv, 1)
                    v.wait_ge(f_lnv, 4 * gt + 1)
                    v.tensor_tensor(lnws2[:, 3:4], lnws2[:, 2:3], lnws2[:, 2:3],
                                    ALU.mult).then_inc(f_lnv, 1)
                    v.tensor_scalar(lnws2[:, 4:5], lnws2[:, 1:2],
                                    consts[:, C_INV_D:C_INV_D + 1], None,
                                    ALU.mult).then_inc(f_lnv, 1)
                    v.wait_ge(f_lnv, 4 * gt + 3)
                    v.tensor_tensor(lnws2[:, 5:6], lnws2[:, 4:5], lnws2[:, 3:4],
                                    ALU.subtract).then_inc(f_lnv, 1)
                    v.wait_ge(f_lnsd, 2 * (gt + 1))
                    v.tensor_tensor(lnws2[:, 8:9], lnws2[:, 2:3], lnws2[:, 7:8],
                                    ALU.mult).then_inc(f_lnw, 1)
                    v.wait_ge(f_lnw, gt + 1)
                    v.tensor_scalar(lnws2[:, 9:10], lnws2[:, 8:9],
                                    consts[:, C_NEG1:C_NEG1 + 1], None,
                                    ALU.mult).then_inc(f_lni, 1)

    return nc




def host_consts_full():
    return host_consts()


def kernel(L0_attn_in_seq, L0_attn_t_k, L0_attn_t_v, L0_attn_t_q, L0_attn_W1,
           L0_attn_b1, L0_attn_W2, L0_attn_b2, L0_ffn_rms_weight, L0_ffn_gate_w,
           L0_ffn_w1, L0_ffn_w2, L0_ffn_w3):
    x = np.ascontiguousarray(np.asarray(L0_attn_in_seq, dtype=np.float32))
    shared = {
        "tkT": np.ascontiguousarray(np.asarray(L0_attn_t_k, np.float32).T),
        "tvT": np.ascontiguousarray(np.asarray(L0_attn_t_v, np.float32).T),
        "tqT": np.ascontiguousarray(np.asarray(L0_attn_t_q, np.float32).T),
        "w1T": np.ascontiguousarray(np.asarray(L0_attn_W1, np.float32).T),
        "w2T": np.ascontiguousarray(np.asarray(L0_attn_W2, np.float32).T),
        "b1c": np.ascontiguousarray(np.asarray(L0_attn_b1, np.float32).reshape(HC, 128).T),
        "b2c": np.ascontiguousarray(np.asarray(L0_attn_b2, np.float32).reshape(DC, 128).T),
        "gate_wt": np.ascontiguousarray(np.asarray(L0_ffn_gate_w, np.float32).T),
        "rms": np.ascontiguousarray(np.asarray(L0_ffn_rms_weight, np.float32).reshape(DC, 128).T),
        "consts": host_consts(),
        "w1t": np.ascontiguousarray(
            np.asarray(L0_ffn_w1, np.float32).transpose(0, 2, 1)
            .reshape(E, DC, 128, 7, 4, 128).transpose(0, 3, 2, 4, 1, 5)
            .reshape(E * 7, 128, DC * 512)),
        "w3t": np.ascontiguousarray(
            np.asarray(L0_ffn_w3, np.float32).transpose(0, 2, 1)
            .reshape(E, DC, 128, 7, 4, 128).transpose(0, 3, 2, 4, 1, 5)
            .reshape(E * 7, 128, DC * 512)),
        "w2t": np.ascontiguousarray(
            np.asarray(L0_ffn_w2, np.float32).transpose(0, 2, 1)
            .reshape(E, NIB, 128, D).transpose(0, 2, 1, 3)
            .reshape(E, 128, NIB * D)),
    }
    nc = build()
    in_maps = []
    for c in range(8):
        m = dict(shared)
        m["xT"] = np.ascontiguousarray(
            x[:, c * NTOK:(c + 1) * NTOK, :].transpose(0, 2, 1)
            .reshape(NSTEP, DC, 128, NTOK).transpose(0, 2, 1, 3)
            .reshape(NSTEP, 128, DC * NTOK))
        in_maps.append(m)
    res = run_bass_kernel_spmd(nc, in_maps, core_ids=list(range(8)))
    parts = [np.asarray(res.results[c]["out"]).reshape(NSTEP, NTOK, D)
             for c in range(8)]
    return np.ascontiguousarray(np.concatenate(parts, axis=1))


# version 15
# speedup vs baseline: 1.1163x; 1.0204x over previous
import sys
import contextlib
import numpy as np

sys.path.insert(0, "/opt/trn_rl_repo")
import concourse.bass as bass
import concourse.mybir as mybir
from concourse.bass_utils import run_bass_kernel_spmd
from concourse.alu_op_type import AluOpType as ALU

F32 = mybir.dt.float32
BF16 = mybir.dt.bfloat16
AF = mybir.ActivationFunctionType

D = 1024
DC = 8
HI = 256
HC = HI // 128
NTOK = 256
NSTEP = 8
E = 8
LR = 1e-3
C0 = 2.0 / (2048.0 * 1024.0)
NEG_LRC0 = -LR * C0

C_EPS_RMS, C_INV_D, C_NEG_BIG, C_ZERO, C_EPS_LN, C_ONE, C_NEG1, C_NLRC0, C_TWO, C_NEGHALF = range(10)
NCI = 10
NGV = 11
T_RUN = NSTEP
CC_STUB = False
PHASE = 7
I_FF = 3584
NIB = I_FF // 128
TC = 512
NTC = (NSTEP * NTOK) // TC
NTILE = NSTEP * NTOK // 128


def host_consts():
    c = np.zeros((128, NCI), np.float32)
    c[:, C_EPS_RMS] = 1e-6
    c[:, C_INV_D] = 1.0 / D
    c[:, C_NEG_BIG] = -1e30
    c[:, C_EPS_LN] = 1e-5
    c[:, C_ONE] = 1.0
    c[:, C_NEG1] = -1.0
    c[:, C_NLRC0] = NEG_LRC0
    c[:, C_TWO] = 2.0
    c[:, C_NEGHALF] = -0.5
    return c


def build(n_cores=8):
    nc = bass.Bass("TRN2", target_bir_lowering=False)
    d_x = nc.dram_tensor("xT", [NSTEP, 128, DC * NTOK], F32, kind="ExternalInput").ap()
    d_tk = nc.dram_tensor("tkT", [D, D], F32, kind="ExternalInput").ap()
    d_tv = nc.dram_tensor("tvT", [D, D], F32, kind="ExternalInput").ap()
    d_tq = nc.dram_tensor("tqT", [D, D], F32, kind="ExternalInput").ap()
    d_w1 = nc.dram_tensor("w1T", [D, HI], F32, kind="ExternalInput").ap()
    d_w2 = nc.dram_tensor("w2T", [HI, D], F32, kind="ExternalInput").ap()
    d_b1 = nc.dram_tensor("b1c", [128, HC], F32, kind="ExternalInput").ap()
    d_b2 = nc.dram_tensor("b2c", [128, DC], F32, kind="ExternalInput").ap()
    d_gate = nc.dram_tensor("gate_wt", [D, E], F32, kind="ExternalInput").ap()
    d_rms = nc.dram_tensor("rms", [128, DC], F32, kind="ExternalInput").ap()
    d_consts = nc.dram_tensor("consts", [128, NCI], F32, kind="ExternalInput").ap()
    d_xnsp = nc.dram_tensor("xnsp", [NTILE, 128, DC * 128], F32,
                            kind="Internal").ap()
    d_hsp = nc.dram_tensor("hsp", [NTILE, 128, DC * 128], BF16,
                           kind="Internal").ap()
    d_w1t = nc.dram_tensor("w1t", [E * 7, 128, DC * 512], F32, kind="ExternalInput").ap()
    d_w3t = nc.dram_tensor("w3t", [E * 7, 128, DC * 512], F32, kind="ExternalInput").ap()
    d_w2t = nc.dram_tensor("w2t", [E, 128, NIB * D], F32, kind="ExternalInput").ap()
    d_w2p = nc.dram_tensor("w2pool", [E, 128, NIB * D], BF16, kind="Internal").ap()
    d_out = nc.dram_tensor("out", [NSTEP * NTOK, D], F32, kind="ExternalOutput").ap()
    cc_in = nc.dram_tensor("cc_in", [128, 4106], BF16, kind="Internal")
    cc_out = nc.dram_tensor("cc_out", [128, 4106], BF16, kind="Internal")

    ident_f = nc.inline_tensor(np.eye(128, dtype=np.float32), name="identf")
    ones_f = nc.inline_tensor(np.ones((1, 128), dtype=np.float32), name="onesf")

    ctx = contextlib.ExitStack()
    with ctx:
        sb = lambda name, shape, dt: ctx.enter_context(nc.sbuf_tensor(name, shape, dt))
        consts = sb("consts_sb", [128, NCI], F32)
        idf = sb("idf", [128, 128], F32)
        idb = sb("idb", [128, 128], BF16)
        rms_sb = sb("rmsw_sb", [128, DC], F32)
        ones_sb = sb("ones_sb", [1, 128], F32)
        cwT = sb("cwT", [8, NSTEP * NTOK], F32)
        mctx = contextlib.ExitStack()
        sbm = lambda name, shape, dt: mctx.enter_context(nc.sbuf_tensor(name, shape, dt))
        gate_rms = sbm("gate_rms", [128, DC * E], F32)
        tk_bf = sbm("tk_bf", [128, DC * D], BF16)
        tv_bf = sbm("tv_bf", [128, DC * D], BF16)
        tq_bf = sbm("tq_bf", [128, DC * D], BF16)
        w1t_f = sbm("w1t_f", [128, DC * HI], F32)
        w1t_bf = sbm("w1t_bf", [128, DC * HI], BF16)
        w2t_f = sbm("w2t_f", [128, HC * D], F32)
        w2t_bf = sbm("w2t_bf", [128, HC * D], BF16)
        w2n_bf = sbm("w2n_bf", [128, DC * HI], BF16)
        b1c = sbm("b1c_sb", [128, HC], F32)
        b2c = sbm("b2c_sb", [128, DC], F32)
        xbuf = sbm("xbuf", [128, DC * NTOK], F32)
        xbf = [sbm(f"xbf{p}", [128, DC * NTOK], BF16) for p in range(2)]
        zbuf = [sbm(f"zbuf{p}", [128, DC * NTOK], F32) for p in range(2)]
        zbf = [sbm(f"zbf{p}", [128, DC * NTOK], BF16) for p in range(2)]
        wstg = [sbm(f"wstg{p}", [128, 2048], BF16) for p in range(2)]
        qbf = sbm("qbf", [128, DC * NTOK], BF16)
        xq = sbm("xq", [128, DC * NTOK], F32)
        ebf = sbm("ebf", [128, DC * NTOK], BF16)
        hbf = sbm("hbf", [128, HC * NTOK], BF16)
        dgel = sbm("dgel", [128, HC * NTOK], F32)
        dla_bf = sbm("dla_bf", [128, HC * NTOK], BF16)
        ztok = sbm("ztok", [128, 2 * D], BF16)
        etok = sbm("etok", [128, 2 * D], BF16)
        htok = sbm("htok", [128, 2 * HI], BF16)
        dltok = sbm("dltok", [128, 2 * HI], BF16)
        h2bf = sbm("h2bf", [128, HC * NTOK], BF16)
        sT = sbm("sT", [128, DC * NTOK], F32)
        gbuf = sbm("gbuf", [128, 4106], BF16)
        xn_tok = sbm("xn_tok", [128, D], F32)
        hpre_tok = sbm("hpre_tok", [128, D], BF16)
        xnstage = sbm("xnstage", [128, DC * NTOK], F32)
        hstage = sbm("hstage", [128, DC * NTOK], BF16)
        lnws = sbm("lnws", [128, 16], F32)
        lnscr = sbm("lnscr", [128, 512], F32)
        lg = sbm("lg", [128, E], F32)
        gm1 = sbm("gm1", [128, E], F32)
        gm2 = sbm("gm2", [128, E], F32)
        gm3 = sbm("gm3", [128, E], F32)
        gws = sbm("gws", [128, 8], F32)
        cwtile = sbm("cwtile", [128, E], F32)

        sem = lambda name: ctx.enter_context(nc.semaphore(name))
        s_init = sem("s_init")
        s_initg = sem("s_initg")
        s_wc = [sem("s_wc0"), sem("s_wc1")]
        s_ws = [sem("s_ws0"), sem("s_ws1")]
        s_xt = sem("s_xt")
        s_xb = [sem("s_xb0"), sem("s_xb1")]
        s_prm = sem("s_prm")
        s_zm = sem("s_zm")
        s_zev = sem("s_zev")
        s_ym = sem("s_ym")
        s_yev = sem("s_yev")
        s_am = sem("s_am")
        s_hev = sem("s_hev")
        s_rm = sem("s_rm")
        s_eev = sem("s_eev")
        s_t1 = sem("s_t1")
        s_tv = sem("s_tv")
        s_dhm = sem("s_dhm")
        s_dhv = sem("s_dhv")
        s_dw2 = sem("s_dw2")
        s_gv2 = sem("s_gv2")
        s_dw1 = sem("s_dw1")
        s_qm = sem("s_qm")
        s_qev = sem("s_qev")
        s_gout = sem("s_gout")
        s_cc = sem("s_cc")
        s_gin = sem("s_gin")
        s_upd = sem("s_upd")
        s_ubf = sem("s_ubf")
        s_w2n = sem("s_w2n")
        s_w2nv = sem("s_w2nv")
        s_a2 = sem("s_a2")
        s_h2 = sem("s_h2")
        s_r2 = sem("s_r2")
        s_sT = sem("s_sT")
        s_lt = sem("s_lt")
        s_lacc = sem("s_lacc")
        s_lvv = sem("s_lvv")
        s_lvr = sem("s_lvr")
        s_le = sem("s_le")
        s_lsd = sem("s_lsd")
        s_li = sem("s_li")
        s_xnt = sem("s_xnt")
        s_rsd = sem("s_rsd")
        s_rin = sem("s_rin")
        s_hpt = sem("s_hpt")
        s_tb = sem("s_tb")
        s_stg = sem("s_stg")
        s_sp = [sem("s_sp0"), sem("s_sp1")]
        s_lgm = sem("s_lgm")
        s_gv = sem("s_gv")
        s_gs = sem("s_gs")
        s_cwt = sem("s_cwt")
        s_cwT = sem("s_cwT")
        s_cwsp = sem("s_cwsp")
        f_w13 = [[sem(f"f_w13_{p}_{q}") for q in range(4)] for p in range(2)]
        f_w2 = sem("f_w2")
        f_htc = sem("f_htc")
        f_atc = sem("f_atc")
        f_ag = sem("f_ag")
        f_sil = sem("f_sil")
        f_u = sem("f_u")
        f_eo = sem("f_eo")
        f_vx = sem("f_vx")
        f_acc = sem("f_acc")
        f_cwbm = sem("f_cwbm")
        f_cwbv = sem("f_cwbv")
        f_crow = sem("f_crow")
        f_lnt = sem("f_lnt")
        f_lnsqm = sem("f_lnsqm")
        f_lnv = sem("f_lnv")
        f_lnsd = sem("f_lnsd")
        f_lnw = sem("f_lnw")
        f_lni = sem("f_lni")
        f_lnn = sem("f_lnn")
        f_st = [sem(f"f_st{i}") for i in range(4)]
        f_acc2 = sem("f_acc2")

        block = ctx.enter_context(nc.Block())

        PBIG = mctx.enter_context(nc.psum_tensor("PBIG", [128, 2048], F32))
        PF4 = mctx.enter_context(nc.psum_tensor("PF4", [128, 512], F32))
        PF5 = mctx.enter_context(nc.psum_tensor("PF5", [128, 512], F32))
        PT = [mctx.enter_context(nc.psum_tensor(f"PT{i}", [128, 1024], BF16))
              for i in range(2)]

        def proj_tile(m):
            return PBIG[:, m * NTOK:(m + 1) * NTOK]

        @block.sync
        def _(s):
            s.dma_start(out=consts[:], in_=d_consts).then_inc(s_init, 16)
            s.dma_start(out=idf[:], in_=ident_f.ap()).then_inc(s_init, 16)
            s.dma_start(out=rms_sb[:], in_=d_rms).then_inc(s_init, 16)
            s.dma_start(out=b1c[:], in_=d_b1).then_inc(s_init, 16)
            s.dma_start(out=b2c[:], in_=d_b2).then_inc(s_init, 16)
            s.dma_start(out=gate_rms[:].rearrange("p (c e) -> p c e", c=DC),
                        in_=d_gate.rearrange("(c p) e -> p c e", p=128, c=DC)
                        ).then_inc(s_init, 16)
            s.dma_start(out=w1t_f[:].rearrange("p (c i) -> p c i", c=DC),
                        in_=d_w1.rearrange("(c p) i -> p c i", p=128, c=DC)
                        ).then_inc(s_init, 16)
            s.dma_start(out=w2t_f[:].rearrange("p (c h) -> p c h", c=HC),
                        in_=d_w2.rearrange("(c p) h -> p c h", p=128, c=HC)
                        ).then_inc(s_init, 16)
            s.dma_start(out=ones_sb[:], in_=ones_f.ap()).then_inc(s_init, 16)

            def xload(t):
                if t >= 1:
                    s.wait_ge(s_qev, 2 * t)
                s.dma_start(out=xbuf[:].rearrange("p (c n) -> p c n", c=DC),
                            in_=d_x[t].rearrange("p (c n) -> p c n", c=DC)
                            ).then_inc(s_xt, 16)

            xload(0)
            for t in range(T_RUN):
                if PHASE < 5:
                    continue
                s.wait_ge(s_gv2, 5 * (t + 1))
                s.dma_start(out=cc_in.ap(), in_=gbuf[:]).then_inc(s_gout, 16)
                s.wait_ge(s_cc, t + 1)
                s.dma_start(out=gbuf[:], in_=cc_out.ap()).then_inc(s_gin, 16)
                if t + 1 < T_RUN:
                    xload(t + 1)
                for jj in range(2):
                    gt = 2 * t + jj
                    s.wait_ge(s_stg, 2 * gt + 1)
                    s.dma_start(
                        out=d_xnsp[gt].rearrange("p (c n) -> p c n", c=DC),
                        in_=xnstage[:].rearrange("p (c n) -> p c n", c=DC)
                        [:, :, jj * 128:(jj + 1) * 128],
                    ).then_inc(s_sp[gt % 2], 16)
                    s.wait_ge(s_stg, 2 * gt + 2)
                    s.dma_start(
                        out=d_hsp[gt].rearrange("p (c n) -> p c n", c=DC),
                        in_=hstage[:].rearrange("p (c n) -> p c n", c=DC)
                        [:, :, jj * 128:(jj + 1) * 128],
                    ).then_inc(s_sp[gt % 2], 16)
            pass

        @block.gpsimd
        def _(g):
            g.dma_start(out=idb[:], in_=ident_f.ap()).then_inc(s_initg, 16)
            g.dma_start(out=tk_bf[:].rearrange("p (c h) -> p c h", c=DC),
                        in_=d_tk.rearrange("(c p) h -> p c h", p=128, c=DC)
                        ).then_inc(s_initg, 16)
            g.dma_start(out=tv_bf[:].rearrange("p (c h) -> p c h", c=DC),
                        in_=d_tv.rearrange("(c p) h -> p c h", p=128, c=DC)
                        ).then_inc(s_initg, 16)
            g.dma_start(out=tq_bf[:].rearrange("p (c h) -> p c h", c=DC),
                        in_=d_tq.rearrange("(c p) h -> p c h", p=128, c=DC)
                        ).then_inc(s_initg, 16)
            def xbload(t):
                if t >= 2:
                    g.wait_ge(s_qm, t - 1)
                g.dma_start(out=xbf[t % 2][:].rearrange("p (c n) -> p c n", c=DC),
                            in_=d_x[t].rearrange("p (c n) -> p c n", c=DC)
                            ).then_inc(s_xb[t % 2], 16)

            xbload(0)
            if T_RUN > 1:
                xbload(1)
            for t in range(T_RUN):
                if t + 2 < T_RUN:
                    xbload(t + 2)
                if PHASE < 5:
                    continue
                g.wait_ge(s_gout, 16 * (t + 1))
                g.collective_compute(
                    "AllReduce", ALU.add, [list(range(n_cores))],
                    ins=[cc_in.ap().opt()], outs=[cc_out.ap().opt()],
                ).then_inc(s_cc)
                g.wait_ge(s_cc, t + 1)
                for j in range(14):
                    w = 14 * t + j
                    if w >= 2:
                        g.wait_ge(s_ws[w % 2], 16 * (w // 2))
                    g.dma_start(out=wstg[w % 2][:],
                                in_=d_w2t[t][:, j * 2048:(j + 1) * 2048]
                                ).then_inc(s_wc[w % 2], 16)
                    g.wait_ge(s_wc[w % 2], 16 * (w // 2 + 1))
                    g.dma_start(out=d_w2p[t][:, j * 2048:(j + 1) * 2048],
                                in_=wstg[w % 2][:]).then_inc(s_ws[w % 2], 16)

        def emit_proj(t_, wsb, t):
            for m in range(DC):
                for k in range(DC):
                    ins = t_.matmul(proj_tile(m),
                                    wsb[:, k * D + m * 128: k * D + m * 128 + 128],
                                    xbf[t % 2][:, k * NTOK:(k + 1) * NTOK],
                                    start=(k == 0), stop=(k == DC - 1))
            return ins

        def emit_w2n_transposes(t_, step):
            for half in range(2):
                for k in range(4 * half, 4 * half + 4):
                    for c in range(HC):
                        idx = (k % 4) * 2 + c
                        ins = t_.transpose(
                            PT[half][:, idx * 128:idx * 128 + 128],
                            w2t_bf[:, c * D + k * 128: c * D + k * 128 + 128], idb[:])
                ins.then_inc(s_w2n, 1)

        def scalar_zevac(s, t):
            s.wait_ge(s_zev, 2 * t + 1)
            s.activation(zbf[t % 2][:], zbuf[t % 2][:], AF.Copy).then_inc(s_zev, 1)

        def vector_zevac(v, t):
            v.wait_ge(s_zm, t + 1)
            if t >= 1:
                v.wait_ge(s_zev, 2 * t)
            if t >= 2:
                v.wait_ge(s_eev, t - 1)
                v.wait_ge(s_t1, 6 * (t - 2) + 2)
            v.tensor_copy(zbuf[t % 2][:], PBIG[:]).then_inc(s_zev, 1)

        @block.tensor
        def _(t_):
            t_.wait_ge(s_init, 9 * 16)
            t_.wait_ge(s_initg, 4 * 16)
            t_.wait_ge(s_prm, 2)
            emit_w2n_transposes(t_, -1)
            t_.wait_ge(s_xb[0], 16)
            emit_proj(t_, tk_bf, 0).then_inc(s_zm, 1)
            for t in range(T_RUN):
                if PHASE < 2:
                    continue
                t_.wait_ge(s_zev, 1 if t == 0 else min(2 * (t + 1) + 1, 2 * T_RUN))
                if t >= 1:
                    t_.wait_ge(s_gv, NGV * 2 * t)
                    t_.wait_ge(s_cwT, 2 * t)
                emit_proj(t_, tv_bf, t).then_inc(s_ym, 1)
                if t >= 1:
                    t_.wait_ge(s_lacc, 8 * t)
                    t_.wait_ge(s_xnt, 2 * t)
                t_.wait_ge(s_zev, 2 * (t + 1))
                for m in range(HC):
                    for k in range(DC):
                        ins = t_.matmul(PF4[:, m * NTOK:(m + 1) * NTOK],
                                        w1t_bf[:, k * HI + m * 128: k * HI + m * 128 + 128],
                                        zbf[t % 2][:, k * NTOK:(k + 1) * NTOK],
                                        start=(k == 0), stop=(k == DC - 1))
                ins.then_inc(s_am, 1)
                if PHASE < 3:
                    continue
                t_.wait_ge(s_yev, t + 1)
                t_.wait_ge(s_hev, 2 * (t + 1))
                for m in range(DC):
                    for k in range(HC):
                        ins = t_.matmul(proj_tile(m),
                                        w2t_bf[:, k * D + m * 128: k * D + m * 128 + 128],
                                        hbf[:, k * NTOK:(k + 1) * NTOK],
                                        start=(k == 0), stop=(k == HC - 1))
                ins.then_inc(s_rm, 1)
                t_.wait_ge(s_zev, 2 * (t + 1))
                t_.wait_ge(s_w2nv, 2 * (t + 1))
                if t >= 1:
                    t_.wait_ge(s_tb, 4 * t)
                    t_.wait_ge(s_stg, 4 * t)
                for half in range(2):
                    for idx in range(8):
                        gidx = idx + half * 8
                        tc4, k = gidx // 8, gidx % 8
                        ins = t_.transpose(PT[half][:, idx * 128:(idx + 1) * 128],
                                           zbf[t % 2][:, k * NTOK + tc4 * 128:
                                                      k * NTOK + tc4 * 128 + 128], idb[:])
                    ins.then_inc(s_t1, 1)
                t_.wait_ge(s_eev, t + 1)
                t_.wait_ge(s_tv, 6 * t + 2)
                for half in range(2):
                    for idx in range(8):
                        gidx = idx + half * 8
                        tc4, k = gidx // 8, gidx % 8
                        ins = t_.transpose(PT[half][:, idx * 128:(idx + 1) * 128],
                                           ebf[:, k * NTOK + tc4 * 128:
                                               k * NTOK + tc4 * 128 + 128], idb[:])
                    ins.then_inc(s_t1, 1)
                t_.wait_ge(s_tv, 6 * t + 3)
                for idx in range(4):
                    tc4, k = idx // HC, idx % HC
                    ins = t_.transpose(PT[0][:, idx * 128:(idx + 1) * 128],
                                       hbf[:, k * NTOK + tc4 * 128:
                                           k * NTOK + tc4 * 128 + 128], idb[:])
                ins.then_inc(s_t1, 1)
                if PHASE < 4:
                    continue
                for m in range(HC):
                    for k in range(DC):
                        ins = t_.matmul(PF4[:, m * NTOK:(m + 1) * NTOK],
                                        w2n_bf[:, k * HI + m * 128: k * HI + m * 128 + 128],
                                        ebf[:, k * NTOK:(k + 1) * NTOK],
                                        start=(k == 0), stop=(k == DC - 1))
                ins.then_inc(s_dhm, 1)
                t_.wait_ge(s_dhv, 2 * t + 1)
                t_.wait_ge(s_tv, 6 * t + 4)
                for idx in range(4):
                    tc4, k = idx // HC, idx % HC
                    ins = t_.transpose(PT[1][:, idx * 128:(idx + 1) * 128],
                                       dla_bf[:, k * NTOK + tc4 * 128:
                                              k * NTOK + tc4 * 128 + 128], idb[:])
                ins.then_inc(s_t1, 1)
                t_.wait_ge(s_tv, 6 * t + 5)
                for mi in range(HC):
                    for nh in range(2):
                        q4 = mi * 2 + nh
                        if q4 > 0:
                            t_.wait_ge(s_gv2, 5 * t + q4)
                        elif t >= 1:
                            t_.wait_ge(s_lacc, 8 * t)
                            t_.wait_ge(s_xnt, 2 * t)
                        for k in range(2):
                            ins = t_.matmul(PF5[:],
                                            htok[:, k * HI + mi * 128: k * HI + mi * 128 + 128],
                                            etok[:, k * D + nh * 512: k * D + nh * 512 + 512],
                                            start=(k == 0), stop=(k == 1))
                        ins.then_inc(s_dw2, 1)
                t_.wait_ge(s_eev, t + 1)
                t_.wait_ge(s_tv, 6 * t + 6)
                for m in range(DC):
                    for k in range(2):
                        ins = t_.matmul(proj_tile(m),
                                        ztok[:, k * D + m * 128: k * D + m * 128 + 128],
                                        dltok[:, k * HI: k * HI + NTOK],
                                        start=(k == 0), stop=(k == 1))
                ins.then_inc(s_dw1, 1)
                if PHASE < 5:
                    continue
                t_.wait_ge(s_gv2, 5 * (t + 1))
                emit_proj(t_, tq_bf, t).then_inc(s_qm, 1)
                if t == 0:
                    t_.wait_ge(s_qev, 2)
                    t_.wait_ge(s_xb[1], 16)
                    emit_proj(t_, tk_bf, 1).then_inc(s_zm, 1)
                if t + 2 < T_RUN:
                    if t == 0:
                        t_.wait_ge(s_zev, 3)
                    else:
                        t_.wait_ge(s_qev, 2 * (t + 1))
                    t_.wait_ge(s_xb[t % 2], 16 * ((t + 2) // 2 + 1))
                    emit_proj(t_, tk_bf, t + 2).then_inc(s_zm, 1)
                t_.wait_ge(s_ubf, 2 * t + 1)
                if PHASE < 6:
                    continue
                t_.wait_ge(s_dhv, 2 * t + 1)
                for m in range(HC):
                    for k in range(DC):
                        ins = t_.matmul(PF4[:, m * NTOK:(m + 1) * NTOK],
                                        w1t_bf[:, k * HI + m * 128: k * HI + m * 128 + 128],
                                        qbf[:, k * NTOK:(k + 1) * NTOK],
                                        start=(k == 0), stop=(k == DC - 1))
                ins.then_inc(s_a2, 1)
                t_.wait_ge(s_ubf, 2 * (t + 1))
                t_.wait_ge(s_tv, 6 * (t + 1))
                emit_w2n_transposes(t_, t)
                if t + 2 < T_RUN:
                    t_.wait_ge(s_zev, 2 * (t + 2) + 1)
                else:
                    t_.wait_ge(s_qev, 2 * (t + 1))
                t_.wait_ge(s_h2, t + 1)
                for m in range(DC):
                    for k in range(HC):
                        ins = t_.matmul(proj_tile(m),
                                        w2t_bf[:, k * D + m * 128: k * D + m * 128 + 128],
                                        h2bf[:, k * NTOK:(k + 1) * NTOK],
                                        start=(k == 0), stop=(k == HC - 1))
                ins.then_inc(s_r2, 1)
                if PHASE < 7:
                    continue
                for jj in range(2):
                    gt = 2 * t + jj
                    t_.wait_ge(s_sT, t + 1)
                    if jj == 0:
                        t_.wait_ge(s_h2, t + 1)
                    else:
                        t_.wait_ge(s_lacc, 4 * gt)
                        t_.wait_ge(s_xnt, gt)
                    for k in range(DC):
                        dst = (PF4 if k < 4 else PF5)[:, (k % 4) * 128:(k % 4) * 128 + 128]
                        ins = t_.transpose(dst, sT[:, k * NTOK + jj * 128:
                                                   k * NTOK + jj * 128 + 128], idf[:])
                    ins.then_inc(s_lt, 1)
                    t_.wait_ge(s_xnt, gt + 1)
                    if gt >= 1:
                        t_.wait_ge(s_stg, 2 * gt - 1)
                    for k in range(DC):
                        dst = PBIG[:, k * 128:(k + 1) * 128]
                        ins = t_.transpose(dst, xn_tok[:, k * 128:(k + 1) * 128], idf[:])
                    ins.then_inc(s_tb, 1)
                    t_.wait_ge(s_hpt, gt + 1)
                    t_.wait_ge(s_w2nv, 2 * (t + 2))
                    if gt >= 1:
                        t_.wait_ge(s_stg, 2 * gt)
                    for k in range(DC):
                        ins = t_.transpose(PT[0][:, k * 128:(k + 1) * 128],
                                           hpre_tok[:, k * 128:(k + 1) * 128], idb[:])
                    ins.then_inc(s_tb, 1)
                    t_.wait_ge(s_stg, 2 * gt + 1)
                    t_.wait_ge(s_gs, 1)
                    if gt >= 1:
                        t_.wait_ge(s_gv, NGV * (gt - 1) + 1)
                    for k in range(DC):
                        ins = t_.matmul(PBIG[:, 1024:1032],
                                        xnstage[:, k * NTOK + jj * 128:
                                                k * NTOK + jj * 128 + 128],
                                        gate_rms[:, k * E:(k + 1) * E],
                                        start=(k == 0), stop=(k == DC - 1))
                    ins.then_inc(s_lgm, 1)
                    t_.wait_ge(s_gv, NGV * (gt + 1))
                    if gt >= 1:
                        t_.wait_ge(s_cwT, gt)
                    t_.transpose(PBIG[0:8, 1536:1664], cwtile[:], idf[:]).then_inc(s_cwt, 1)

        @block.scalar
        def _(s):
            s.wait_ge(s_init, 9 * 16)
            for k in range(DC):
                ins = s.activation(gate_rms[:, k * E:(k + 1) * E],
                                   gate_rms[:, k * E:(k + 1) * E],
                                   AF.Copy, scale=rms_sb[:, k:k + 1])
            ins.then_inc(s_gs, 1)
            s.activation(w1t_bf[:], w1t_f[:], AF.Copy).then_inc(s_prm, 1)
            s.activation(w2t_bf[:], w2t_f[:], AF.Copy).then_inc(s_prm, 1)
            scalar_zevac(s, 0)
            for t in range(T_RUN):
                if PHASE < 2:
                    continue
                s.wait_ge(s_am, t + 1)
                if t >= 1:
                    s.wait_ge(s_dhv, 2 * (t - 1) + 1)
                    s.wait_ge(s_t1, 6 * (t - 1) + 5)
                for m in range(HC):
                    s.activation(hbf[:, m * NTOK:(m + 1) * NTOK],
                                 PF4[:, m * NTOK:(m + 1) * NTOK], AF.Gelu,
                                 bias=b1c[:, m:m + 1])
                for m in range(HC):
                    ins = s.activation(dgel[:, m * NTOK:(m + 1) * NTOK],
                                       PF4[:, m * NTOK:(m + 1) * NTOK],
                                       AF.Derivative_Gelu, bias=b1c[:, m:m + 1])
                ins.then_inc(s_hev, 2)
                if PHASE < 3:
                    continue

                if PHASE < 4:
                    continue

                if PHASE < 5:
                    continue
                s.wait_ge(s_qm, t + 1)
                s.wait_ge(s_qev, 2 * t + 1)
                if t >= 1:
                    s.wait_ge(s_a2, t)
                s.activation(qbf[:], PBIG[:], AF.Copy).then_inc(s_qev, 1)
                if t == 0:
                    scalar_zevac(s, 1)
                if t + 2 < T_RUN:
                    scalar_zevac(s, t + 2)
                s.wait_ge(s_upd, 4 * t + 1)
                s.activation(w1t_bf[:], w1t_f[:], AF.Copy).then_inc(s_ubf, 1)
                s.wait_ge(s_upd, 4 * t + 2)
                s.activation(w2t_bf[:], w2t_f[:], AF.Copy).then_inc(s_ubf, 1)
                if PHASE < 6:
                    continue
                s.wait_ge(s_upd, 4 * t + 3)
                s.wait_ge(s_a2, t + 1)
                for m in range(HC):
                    ins = s.activation(h2bf[:, m * NTOK:(m + 1) * NTOK],
                                       PF4[:, m * NTOK:(m + 1) * NTOK], AF.Gelu,
                                       bias=b1c[:, m:m + 1])
                ins.then_inc(s_h2, 1)
                if PHASE < 7:
                    continue
                for jj in range(2):
                    gt = 2 * t + jj
                    s.wait_ge(s_lt, gt + 1)
                    if gt >= 1:
                        s.wait_ge(s_lvv, gt)
                    s.activation(lnscr[:], PF4[:], AF.Copy,
                                 accum_out=lnws[:, 0:1]).then_inc(s_lacc, 1)
                    s.wait_ge(s_lacc, 4 * gt + 1)
                    s.activation(lnscr[:], PF5[:], AF.Copy,
                                 accum_out=lnws[:, 1:2]).then_inc(s_lacc, 1)
                    s.wait_ge(s_lacc, 4 * gt + 2)
                    s.activation(lnscr[:], PF4[:], AF.Square,
                                 accum_out=lnws[:, 2:3]).then_inc(s_lacc, 1)
                    s.wait_ge(s_lacc, 4 * gt + 3)
                    s.activation(lnscr[:], PF5[:], AF.Square,
                                 accum_out=lnws[:, 3:4]).then_inc(s_lacc, 1)
                    s.wait_ge(s_lvv, gt + 1)
                    s.activation(lnws[:, 8:9], lnws[:, 7:8], AF.Ln,
                                 bias=consts[:, C_EPS_LN:C_EPS_LN + 1]).then_inc(s_lsd, 1)
                    s.wait_ge(s_lsd, 2 * gt + 1)
                    s.activation(lnws[:, 9:10], lnws[:, 8:9], AF.Exp,
                                 scale=consts[:, C_NEGHALF:C_NEGHALF + 1]).then_inc(s_lsd, 1)
                    s.wait_ge(s_li, gt + 1)
                    if gt >= 1:
                        s.wait_ge(s_tb, 2 * gt - 1)
                        s.wait_ge(s_hpt, gt)
                    s.activation(xn_tok[:, 0:512], PF4[:], AF.Identity,
                                 scale=lnws[:, 9:10], bias=lnws[:, 10:11])
                    ins = s.activation(xn_tok[:, 512:1024], PF5[:], AF.Identity,
                                       scale=lnws[:, 9:10], bias=lnws[:, 10:11])
                    ins.then_inc(s_xnt, 1)
                    s.wait_ge(s_lvr, gt + 1)
                    s.activation(lnws[:, 14:15], lnws[:, 13:14], AF.Ln,
                                 bias=consts[:, C_EPS_RMS:C_EPS_RMS + 1]).then_inc(s_rsd, 1)
                    s.wait_ge(s_rsd, 2 * gt + 1)
                    s.activation(lnws[:, 15:16], lnws[:, 14:15], AF.Exp,
                                 scale=consts[:, C_NEGHALF:C_NEGHALF + 1]).then_inc(s_rsd, 1)
                    s.wait_ge(s_rsd, 2 * (gt + 1))
                    if gt >= 1:
                        s.wait_ge(s_tb, 2 * gt)
                    s.activation(hpre_tok[:], xn_tok[:], AF.Copy,
                                 scale=lnws[:, 15:16]).then_inc(s_hpt, 1)
                    s.wait_ge(s_gv, NGV * gt + 2)
                    s.activation(gm3[:], lg[:], AF.Exp, bias=gws[:, 0:1]).then_inc(s_gs, 1)
                    s.wait_ge(s_gv, NGV * gt + 7)
                    s.activation(gws[:, 3:4], gws[:, 2:3], AF.Exp).then_inc(s_gs, 1)

        @block.vector
        def _(v):
            self_edge = [0]
            for half in range(2):
                v.wait_ge(s_w2n, half + 1)
                v.tensor_copy(w2n_bf[:, half * 1024:(half + 1) * 1024],
                              PT[half][:]).then_inc(s_w2nv, 1)
            vector_zevac(v, 0)
            for t in range(T_RUN):
                if PHASE < 2:
                    continue
                v.wait_ge(s_ym, t + 1)
                v.wait_ge(s_zev, 2 * (t + 1))
                for m in range(DC):
                    ins = v.scalar_tensor_tensor(zbuf[t % 2][:, m * NTOK:(m + 1) * NTOK],
                                                 zbuf[t % 2][:, m * NTOK:(m + 1) * NTOK],
                                                 b2c[:, m:m + 1], proj_tile(m),
                                                 ALU.add, ALU.subtract)
                ins.then_inc(s_yev, 1)
                if PHASE < 3:
                    continue
                v.wait_ge(s_rm, t + 1)
                v.wait_ge(s_yev, t + 1)
                if t >= 1:
                    v.wait_ge(s_dhm, t)
                    v.wait_ge(s_t1, 6 * (t - 1) + 4)
                v.tensor_tensor(ebf[:], zbuf[t % 2][:], PBIG[:],
                                ALU.add).then_inc(s_eev, 1)
                v.wait_ge(s_t1, 6 * t + 1)
                v.tensor_copy(ztok[:, 0:1024], PT[0][:]).then_inc(s_tv, 1)
                v.wait_ge(s_t1, 6 * t + 2)
                v.tensor_copy(ztok[:, 1024:2048], PT[1][:]).then_inc(s_tv, 1)
                v.wait_ge(s_t1, 6 * t + 3)
                v.tensor_copy(etok[:, 0:1024], PT[0][:]).then_inc(s_tv, 1)
                v.wait_ge(s_t1, 6 * t + 4)
                v.tensor_copy(etok[:, 1024:2048], PT[1][:]).then_inc(s_tv, 1)
                v.wait_ge(s_t1, 6 * t + 5)
                v.tensor_copy(htok[:], PT[0][:, 0:512]).then_inc(s_tv, 1)
                if PHASE < 4:
                    continue
                v.wait_ge(s_dhm, t + 1)
                if t >= 1:
                    v.wait_ge(s_t1, 6 * (t - 1) + 6)
                for m in range(HC):
                    ins = v.tensor_tensor(dla_bf[:, m * NTOK:(m + 1) * NTOK],
                                          dgel[:, m * NTOK:(m + 1) * NTOK],
                                          PF4[:, m * NTOK:(m + 1) * NTOK], ALU.mult)
                ins.then_inc(s_dhv, 1)
                v.wait_ge(s_dhv, 2 * t + 1)
                v.reduce_sum(lnscr[:, 16:17], dla_bf[:, 0:NTOK],
                             axis=mybir.AxisListType.X)
                v.reduce_sum(lnscr[:, 17:18], dla_bf[:, NTOK:2 * NTOK],
                             axis=mybir.AxisListType.X).then_inc(s_le, 1)
                self_edge[0] += 1
                v.wait_ge(s_le, self_edge[0])
                ins = v.tensor_copy(gbuf[:, 4096:4098], lnscr[:, 16:18])
                ins.then_inc(s_dhv, 1)
                v.wait_ge(s_t1, 6 * t + 6)
                v.tensor_copy(dltok[:], PT[1][:, 0:512]).then_inc(s_tv, 1)
                for q4 in range(4):
                    mi, nh = q4 // 2, q4 % 2
                    v.wait_ge(s_dw2, 4 * t + q4 + 1)
                    ins = v.tensor_copy(
                        gbuf[:, 2048 + mi * 1024 + nh * 512:
                             2048 + mi * 1024 + nh * 512 + 512], PF5[:])
                    ins.then_inc(s_gv2, 1)
                for m in range(DC):
                    ins = v.reduce_sum(lnscr[:, 20 + m:21 + m],
                                       ebf[:, m * NTOK:(m + 1) * NTOK],
                                       axis=mybir.AxisListType.X)
                ins.then_inc(s_le, 1)
                self_edge[0] += 1
                v.wait_ge(s_le, self_edge[0])
                v.tensor_copy(gbuf[:, 4098:4106], lnscr[:, 20:28])
                v.wait_ge(s_dw1, t + 1)
                v.tensor_copy(gbuf[:, 0:2048], PBIG[:]).then_inc(s_gv2, 1)
                if PHASE < 5:
                    continue
                v.wait_ge(s_qm, t + 1)
                v.wait_ge(s_xt, 16 * (t + 1))
                if t >= 1:
                    v.wait_ge(s_sT, t)
                v.tensor_tensor(xq[:], PBIG[:], xbuf[:], ALU.add).then_inc(s_qev, 1)
                if t == 0:
                    vector_zevac(v, 1)
                if t + 2 < T_RUN:
                    vector_zevac(v, t + 2)
                v.wait_ge(s_gin, 16 * (t + 1))
                v.wait_ge(s_ubf, 2 * t)
                v.scalar_tensor_tensor(w1t_f[:], gbuf[:, 0:2048],
                                       consts[:, C_NLRC0:C_NLRC0 + 1], w1t_f[:],
                                       ALU.mult, ALU.add).then_inc(s_upd, 1)
                v.scalar_tensor_tensor(w2t_f[:], gbuf[:, 2048:4096],
                                       consts[:, C_NLRC0:C_NLRC0 + 1], w2t_f[:],
                                       ALU.mult, ALU.add).then_inc(s_upd, 1)
                v.scalar_tensor_tensor(b1c[:], gbuf[:, 4096:4098],
                                       consts[:, C_NLRC0:C_NLRC0 + 1], b1c[:],
                                       ALU.mult, ALU.add).then_inc(s_upd, 1)
                v.scalar_tensor_tensor(b2c[:], gbuf[:, 4098:4106],
                                       consts[:, C_NLRC0:C_NLRC0 + 1], b2c[:],
                                       ALU.mult, ALU.add).then_inc(s_upd, 1)
                for half in range(2):
                    v.wait_ge(s_w2n, 2 * (t + 1) + half + 1)
                    v.tensor_copy(w2n_bf[:, half * 1024:(half + 1) * 1024],
                                  PT[half][:]).then_inc(s_w2nv, 1)
                if PHASE < 6:
                    continue
                v.wait_ge(s_r2, t + 1)
                v.wait_ge(s_upd, 4 * (t + 1))
                v.wait_ge(s_qev, 2 * (t + 1))
                for m in range(DC):
                    ins = v.scalar_tensor_tensor(sT[:, m * NTOK:(m + 1) * NTOK],
                                                 proj_tile(m), b2c[:, m:m + 1],
                                                 xq[:, m * NTOK:(m + 1) * NTOK],
                                                 ALU.add, ALU.add)
                ins.then_inc(s_sT, 1)
                if PHASE < 7:
                    continue
                for jj in range(2):
                    gt = 2 * t + jj

                    def vtick(ins):
                        self_edge[0] += 1
                        ins.then_inc(s_le, 1)

                    def vedge():
                        v.wait_ge(s_le, self_edge[0])

                    v.wait_ge(s_lacc, 4 * (gt + 1))
                    vtick(v.tensor_tensor(lnws[:, 4:5], lnws[:, 0:1], lnws[:, 1:2],
                                          ALU.add))
                    vedge()
                    vtick(v.tensor_scalar(lnws[:, 5:6], lnws[:, 4:5],
                                          consts[:, C_INV_D:C_INV_D + 1], None,
                                          ALU.mult))
                    vtick(v.tensor_tensor(lnws[:, 6:7], lnws[:, 2:3], lnws[:, 3:4],
                                          ALU.add))
                    vedge()
                    vtick(v.tensor_scalar(lnws[:, 6:7], lnws[:, 6:7],
                                          consts[:, C_INV_D:C_INV_D + 1], None,
                                          ALU.mult))
                    vtick(v.tensor_tensor(lnscr[:, 0:1], lnws[:, 5:6], lnws[:, 5:6],
                                          ALU.mult))
                    vedge()
                    v.tensor_tensor(lnws[:, 7:8], lnws[:, 6:7], lnscr[:, 0:1],
                                    ALU.subtract).then_inc(s_lvv, 1)
                    v.wait_ge(s_lsd, 2 * (gt + 1))
                    vtick(v.tensor_tensor(lnscr[:, 1:2], lnws[:, 5:6], lnws[:, 9:10],
                                          ALU.mult))
                    vedge()
                    v.tensor_scalar(lnws[:, 10:11], lnscr[:, 1:2],
                                    consts[:, C_NEG1:C_NEG1 + 1], None,
                                    ALU.mult).then_inc(s_li, 1)
                    if gt >= 1:
                        v.wait_ge(s_rsd, 2 * gt - 1)
                    vtick(v.tensor_tensor(lnws[:, 11:12], lnws[:, 9:10], lnws[:, 9:10],
                                          ALU.mult))
                    vedge()
                    v.tensor_tensor(lnws[:, 13:14], lnws[:, 7:8], lnws[:, 11:12],
                                    ALU.mult).then_inc(s_lvr, 1)
                    v.wait_ge(s_tb, 2 * gt + 1)
                    if gt >= 1:
                        v.wait_ge(s_sp[(gt - 1) % 2], 32 * ((gt - 1) // 2 + 1))
                    if gt >= 2:
                        v.wait_ge(s_sp[gt % 2], 32 * (gt // 2))
                    dst = xnstage[:].rearrange("p (k t) -> p k t", k=DC)[
                        :, :, jj * 128:jj * 128 + 128]
                    ins = v.tensor_copy(dst, PBIG[:, 0:1024]
                                        .rearrange("p (k t) -> p k t", k=DC))
                    ins.then_inc(s_stg, 1)
                    v.wait_ge(s_tb, 2 * gt + 2)
                    for k in range(DC):
                        ins = v.tensor_scalar(
                            hstage[:, k * NTOK + jj * 128: k * NTOK + jj * 128 + 128],
                            PT[0][:, k * 128:(k + 1) * 128],
                            rms_sb[:, k:k + 1], None, ALU.mult)
                    ins.then_inc(s_stg, 1)
                    v.wait_ge(s_lgm, gt + 1)
                    v.tensor_copy(lg[:], PBIG[:, 1024:1032]).then_inc(s_gv, 1)
                    v.wait_ge(s_gv, NGV * gt + 1)
                    v.reduce_max(gws[:, 0:1], lg[:], axis=mybir.AxisListType.X,
                                 negate=True).then_inc(s_gv, 1)
                    v.wait_ge(s_gv, NGV * gt + 2)
                    v.tensor_scalar(gm1[:], lg[:], gws[:, 0:1],
                                    consts[:, C_ZERO:C_ZERO + 1],
                                    ALU.add, ALU.is_equal).then_inc(s_gv, 1)
                    v.wait_ge(s_gv, NGV * gt + 3)
                    v.scalar_tensor_tensor(gm2[:], gm1[:],
                                           consts[:, C_NEG_BIG:C_NEG_BIG + 1],
                                           lg[:], ALU.mult, ALU.add).then_inc(s_gv, 1)
                    v.wait_ge(s_gv, NGV * gt + 4)
                    v.reduce_max(gws[:, 1:2], gm2[:], axis=mybir.AxisListType.X,
                                 negate=True).then_inc(s_gv, 1)
                    v.wait_ge(s_gv, NGV * gt + 5)
                    v.tensor_scalar(gm2[:], lg[:], gws[:, 1:2],
                                    consts[:, C_ZERO:C_ZERO + 1],
                                    ALU.add, ALU.is_ge).then_inc(s_gv, 1)
                    v.tensor_tensor(gws[:, 2:3], gws[:, 0:1], gws[:, 1:2],
                                    ALU.subtract).then_inc(s_gv, 1)
                    v.wait_ge(s_gs, 2 * gt + 3)
                    v.tensor_scalar(gws[:, 4:5], gws[:, 3:4],
                                    consts[:, C_ONE:C_ONE + 1], None,
                                    ALU.add).then_inc(s_gv, 1)
                    v.wait_ge(s_gv, NGV * gt + 8)
                    v.reciprocal(gws[:, 5:6], gws[:, 4:5]).then_inc(s_gv, 1)
                    v.wait_ge(s_gs, 2 * gt + 2)
                    v.tensor_tensor(gm1[:], gm2[:], gm3[:], ALU.mult).then_inc(s_gv, 1)
                    v.wait_ge(s_gv, NGV * gt + 10)
                    if gt >= 1:
                        v.wait_ge(s_cwt, gt)
                    v.tensor_scalar(cwtile[:], gm1[:], gws[:, 5:6], None,
                                    ALU.mult).then_inc(s_gv, 1)
                    v.wait_ge(s_cwt, gt + 1)
                    v.tensor_copy(cwT[0:8, gt * 128:(gt + 1) * 128],
                                  PBIG[0:8, 1536:1664]).then_inc(s_cwT, 1)


        mctx.close()

        fsb = lambda name, shape, dt: ctx.enter_context(nc.sbuf_tensor(name, shape, dt))
        cwrow = fsb("cwrow", [1, TC], F32)
        cwb = fsb("cwb", [128, TC], F32)
        hbuf2 = fsb("hbuf2", [128, DC * TC], BF16)
        acc = fsb("acc", [128, DC * TC], F32)
        ubuf = fsb("ubuf", [128, NIB * TC], BF16)
        w2sb = fsb("w2sb", [128, NIB * D], BF16)
        wb1 = [fsb(f"wb1_{p}", [128, DC * 512], BF16) for p in range(2)]
        wb3 = [fsb(f"wb3_{p}", [128, DC * 512], BF16) for p in range(2)]
        stmp = [fsb(f"stmp_{p}", [128, TC], F32) for p in range(2)]
        vtmp = fsb("vtmp", [128, TC], F32)
        outsb = [fsb(f"outsb_{p}", [128, D], F32) for p in range(4)]
        lnsq2 = fsb("lnsq2", [128, D], F32)
        lnws2 = fsb("lnws2", [128, 10], F32)

        pa = [ctx.enter_context(nc.psum_tensor(f"pa{p}", [128, TC], F32)) for p in range(2)]
        pg = [ctx.enter_context(nc.psum_tensor(f"pg{p}", [128, TC], F32)) for p in range(2)]
        peo = [ctx.enter_context(nc.psum_tensor(f"peo{p}", [128, TC], F32)) for p in range(2)]
        pln = ctx.enter_context(nc.psum_tensor("pln", [128, D], F32))

        def GE(tc, e):
            return tc * E + e

        @block.gpsimd
        def _(g):
            g.wait_ge(s_cwT, NTILE)
            g.wait_ge(s_sp[0], 32 * NSTEP)
            g.wait_ge(s_sp[1], 32 * NSTEP)
            g.wait_ge(s_ws[0], 16 * 56)
            g.wait_ge(s_ws[1], 16 * 56)
            for tc in range(NTC):
                for e in range(E):
                    ge = GE(tc, e)
                    for ic in range(7):
                        gch = ge * 7 + ic
                        if gch >= 2:
                            g.wait_ge(f_ag, 2 * (4 * (gch - 2) + 4))
                        for q in range(4):
                            g.dma_start(out=wb1[gch % 2][:, q * 1024:(q + 1) * 1024],
                                        in_=d_w1t[e * 7 + ic][:, q * 1024:(q + 1) * 1024]
                                        ).then_inc(f_w13[gch % 2][q], 16)
                            g.dma_start(out=wb3[gch % 2][:, q * 1024:(q + 1) * 1024],
                                        in_=d_w3t[e * 7 + ic][:, q * 1024:(q + 1) * 1024]
                                        ).then_inc(f_w13[gch % 2][q], 16)

        @block.sync
        def _(s):
            s.wait_ge(s_cwT, NTILE)
            s.wait_ge(s_sp[0], 32 * NSTEP)
            s.wait_ge(s_sp[1], 32 * NSTEP)
            s.wait_ge(s_ws[0], 16 * 56)
            s.wait_ge(s_ws[1], 16 * 56)

            def load_tc(tc):
                s.wait_ge(s_sp[0], 32 * 2 * (tc + 1))
                s.wait_ge(s_sp[1], 32 * 2 * (tc + 1))
                if tc >= 1:
                    s.wait_ge(f_ag, 2 * 28 * 8 * tc)
                for q in range(4):
                    s.dma_start(out=hbuf2[:].rearrange("p (c t) -> p c t", c=DC)
                                [:, :, q * 128:(q + 1) * 128],
                                in_=d_hsp[tc * 4 + q].rearrange("p (c n) -> p c n", c=DC)
                                ).then_inc(f_htc, 16)
                if tc >= 1:
                    s.wait_ge(f_lnt, 4 * tc)
                for q in range(4):
                    s.dma_start(out=acc[:].rearrange("p (c t) -> p c t", c=DC)
                                [:, :, q * 128:(q + 1) * 128],
                                in_=d_xnsp[tc * 4 + q].rearrange("p (c n) -> p c n", c=DC)
                                ).then_inc(f_atc, 16)

            def slot(tc, e):
                ge = GE(tc, e)
                s.wait_ge(f_cwbm, ge)
                s.dma_start(out=cwrow[:],
                            in_=cwT[e:e + 1, tc * TC:(tc + 1) * TC]).then_inc(f_crow, 16)
                if ge >= 1:
                    s.wait_ge(f_eo, 8 * ge)
                for piece in range(NIB):
                    s.dma_start(out=w2sb[:, piece * D:(piece + 1) * D],
                                in_=d_w2p[e][:, piece * D:(piece + 1) * D]
                                ).then_inc(f_w2, 16)

            load_tc(0)
            for tc in range(NTC):
                for e in range(E):
                    if tc >= 1 and e == 0:
                        continue
                    slot(tc, e)
                if tc + 1 < NTC:
                    load_tc(tc + 1)
                    slot(tc + 1, 0)
                for jj in range(4):
                    gt = tc * 4 + jj
                    s.wait_ge(f_lnn, gt + 1)
                    s.dma_start(out=d_out[gt * 128:(gt + 1) * 128, :],
                                in_=outsb[gt % 4][:]).then_inc(f_st[gt % 4], 16)
            for q in range(4):
                s.wait_ge(f_st[q], 16 * (NTILE // 4))

        @block.tensor
        def _(t_):
            t_.wait_ge(s_cwT, NTILE)
            for tc in range(NTC):
                t_.wait_ge(f_htc, 64 * (tc + 1))
                for e in range(E):
                    ge = GE(tc, e)
                    t_.wait_ge(f_crow, 16 * (ge + 1))
                    if ge >= 1:
                        t_.wait_ge(f_acc, 8 * (ge - 1) + 7)
                    t_.matmul(peo[0][:], ones_sb[:], cwrow[:],
                              start=True, stop=True).then_inc(f_cwbm, 1)
                    for ib in range(NIB):
                        gib = ge * NIB + ib
                        gch = ge * 7 + ib // 4
                        t_.wait_ge(f_w13[gch % 2][ib % 4], 32 * (gch // 2 + 1))
                        if gib >= 2:
                            t_.wait_ge(f_sil, gib - 1)
                        for k in range(DC):
                            ins = t_.matmul(
                                pa[ib % 2][:],
                                wb1[gch % 2][:, (ib % 4) * 1024 + k * 128:
                                             (ib % 4) * 1024 + k * 128 + 128],
                                hbuf2[:, k * TC:(k + 1) * TC],
                                start=(k == 0), stop=(k == DC - 1))
                        ins.then_inc(f_ag, 1)
                        if gib >= 2:
                            t_.wait_ge(f_u, gib - 1)
                        for k in range(DC):
                            ins = t_.matmul(
                                pg[ib % 2][:],
                                wb3[gch % 2][:, (ib % 4) * 1024 + k * 128:
                                             (ib % 4) * 1024 + k * 128 + 128],
                                hbuf2[:, k * TC:(k + 1) * TC],
                                start=(k == 0), stop=(k == DC - 1))
                        ins.then_inc(f_ag, 1)
                    for h in range(DC):
                        gh = ge * 8 + h
                        if h == 0:
                            t_.wait_ge(f_u, NIB * (ge + 1))
                            t_.wait_ge(f_w2, 448 * (ge + 1))
                            t_.wait_ge(f_cwbv, ge + 1)
                        if h == 1 and ge >= 1:
                            t_.wait_ge(f_acc, 8 * ge)
                        if h >= 2:
                            t_.wait_ge(f_acc, gh - 1)
                        for ib in range(NIB):
                            ins = t_.matmul(peo[h % 2][:],
                                            w2sb[:, ib * D + h * 128: ib * D + h * 128 + 128],
                                            ubuf[:, ib * TC:(ib + 1) * TC],
                                            start=(ib == 0), stop=(ib == NIB - 1))
                        ins.then_inc(f_eo, 1)
                for jj in range(4):
                    gt = tc * 4 + jj
                    t_.wait_ge(f_acc, 64 * (tc + 1))
                    if gt >= 1:
                        t_.wait_ge(f_lnn, gt)
                        t_.wait_ge(f_lnsqm, 2 * gt)
                    for k in range(DC):
                        ins = t_.transpose(pln[:, k * 128:(k + 1) * 128],
                                           acc[:, k * TC + jj * 128: k * TC + jj * 128 + 128],
                                           idf[:])
                    ins.then_inc(f_lnt, 1)

        @block.scalar
        def _(s):
            for tc in range(NTC):
                for e in range(E):
                    ge = GE(tc, e)
                    for ib in range(NIB):
                        gib = ge * NIB + ib
                        s.wait_ge(f_ag, 2 * gib + 1)
                        if gib >= 2:
                            s.wait_ge(f_u, gib - 1)
                        s.activation(stmp[ib % 2][:], pa[ib % 2][:],
                                     AF.Silu).then_inc(f_sil, 1)
                for jj in range(4):
                    gt = tc * 4 + jj
                    s.wait_ge(f_lnt, gt + 1)
                    if gt >= 1:
                        s.wait_ge(f_lnv, 4 * gt)
                    s.activation(lnsq2[:], pln[:], AF.Copy,
                                 accum_out=lnws2[:, 0:1]).then_inc(f_lnsqm, 1)
                    s.wait_ge(f_lnsqm, 2 * gt + 1)
                    s.activation(lnsq2[:], pln[:], AF.Square,
                                 accum_out=lnws2[:, 1:2]).then_inc(f_lnsqm, 1)
                    s.wait_ge(f_lnv, 4 * (gt + 1))
                    s.activation(lnws2[:, 6:7], lnws2[:, 5:6], AF.Ln,
                                 bias=consts[:, C_EPS_LN:C_EPS_LN + 1]).then_inc(f_lnsd, 1)
                    s.wait_ge(f_lnsd, 2 * gt + 1)
                    s.activation(lnws2[:, 7:8], lnws2[:, 6:7], AF.Exp,
                                 scale=consts[:, C_NEGHALF:C_NEGHALF + 1]).then_inc(f_lnsd, 1)
                    s.wait_ge(f_lni, gt + 1)
                    if gt >= 4:
                        s.wait_ge(f_st[gt % 4], 16 * (gt // 4))
                    s.activation(outsb[gt % 4][:], pln[:], AF.Identity,
                                 scale=lnws2[:, 7:8], bias=lnws2[:, 9:10]).then_inc(f_lnn, 1)

        @block.vector
        def _(v):
            for tc in range(NTC):
                v.wait_ge(f_atc, 64 * (tc + 1))
                v.tensor_scalar(acc[:], acc[:], consts[:, C_TWO:C_TWO + 1], None,
                                ALU.mult).then_inc(f_acc2, 1)
                for e in range(E):
                    ge = GE(tc, e)
                    v.wait_ge(f_cwbm, ge + 1)
                    v.tensor_copy(cwb[:], peo[0][:]).then_inc(f_cwbv, 1)
                    for ib in range(NIB):
                        gib = ge * NIB + ib
                        if ib == 0 and ge >= 1:
                            v.wait_ge(f_eo, 8 * ge)
                        v.wait_ge(f_sil, gib + 1)
                        v.wait_ge(f_ag, 2 * gib + 2)
                        v.tensor_tensor(ubuf[:, ib * TC:(ib + 1) * TC],
                                        stmp[ib % 2][:], pg[ib % 2][:],
                                        ALU.mult).then_inc(f_u, 1)
                    for h in range(DC):
                        gh = ge * 8 + h
                        v.wait_ge(f_eo, gh + 1)
                        v.wait_ge(f_acc2, tc + 1)
                        if gh >= 1:
                            v.wait_ge(f_acc, gh)
                        v.tensor_tensor(vtmp[:], peo[h % 2][:], cwb[:],
                                        ALU.mult).then_inc(f_vx, 1)
                        v.wait_ge(f_vx, gh + 1)
                        v.tensor_tensor(acc[:, h * TC:(h + 1) * TC],
                                        acc[:, h * TC:(h + 1) * TC], vtmp[:],
                                        ALU.add).then_inc(f_acc, 1)
                for jj in range(4):
                    gt = tc * 4 + jj
                    v.wait_ge(f_lnsqm, 2 * (gt + 1))
                    v.tensor_scalar(lnws2[:, 2:3], lnws2[:, 0:1],
                                    consts[:, C_INV_D:C_INV_D + 1], None,
                                    ALU.mult).then_inc(f_lnv, 1)
                    v.wait_ge(f_lnv, 4 * gt + 1)
                    v.tensor_tensor(lnws2[:, 3:4], lnws2[:, 2:3], lnws2[:, 2:3],
                                    ALU.mult).then_inc(f_lnv, 1)
                    v.tensor_scalar(lnws2[:, 4:5], lnws2[:, 1:2],
                                    consts[:, C_INV_D:C_INV_D + 1], None,
                                    ALU.mult).then_inc(f_lnv, 1)
                    v.wait_ge(f_lnv, 4 * gt + 3)
                    v.tensor_tensor(lnws2[:, 5:6], lnws2[:, 4:5], lnws2[:, 3:4],
                                    ALU.subtract).then_inc(f_lnv, 1)
                    v.wait_ge(f_lnsd, 2 * (gt + 1))
                    v.tensor_tensor(lnws2[:, 8:9], lnws2[:, 2:3], lnws2[:, 7:8],
                                    ALU.mult).then_inc(f_lnw, 1)
                    v.wait_ge(f_lnw, gt + 1)
                    v.tensor_scalar(lnws2[:, 9:10], lnws2[:, 8:9],
                                    consts[:, C_NEG1:C_NEG1 + 1], None,
                                    ALU.mult).then_inc(f_lni, 1)

    return nc




def host_consts_full():
    return host_consts()


def kernel(L0_attn_in_seq, L0_attn_t_k, L0_attn_t_v, L0_attn_t_q, L0_attn_W1,
           L0_attn_b1, L0_attn_W2, L0_attn_b2, L0_ffn_rms_weight, L0_ffn_gate_w,
           L0_ffn_w1, L0_ffn_w2, L0_ffn_w3):
    x = np.ascontiguousarray(np.asarray(L0_attn_in_seq, dtype=np.float32))
    shared = {
        "tkT": np.ascontiguousarray(np.asarray(L0_attn_t_k, np.float32).T),
        "tvT": np.ascontiguousarray(np.asarray(L0_attn_t_v, np.float32).T),
        "tqT": np.ascontiguousarray(np.asarray(L0_attn_t_q, np.float32).T),
        "w1T": np.ascontiguousarray(np.asarray(L0_attn_W1, np.float32).T),
        "w2T": np.ascontiguousarray(np.asarray(L0_attn_W2, np.float32).T),
        "b1c": np.ascontiguousarray(np.asarray(L0_attn_b1, np.float32).reshape(HC, 128).T),
        "b2c": np.ascontiguousarray(np.asarray(L0_attn_b2, np.float32).reshape(DC, 128).T),
        "gate_wt": np.ascontiguousarray(np.asarray(L0_ffn_gate_w, np.float32).T),
        "rms": np.ascontiguousarray(np.asarray(L0_ffn_rms_weight, np.float32).reshape(DC, 128).T),
        "consts": host_consts(),
        "w1t": np.ascontiguousarray(
            np.asarray(L0_ffn_w1, np.float32).transpose(0, 2, 1)
            .reshape(E, DC, 128, 7, 4, 128).transpose(0, 3, 2, 4, 1, 5)
            .reshape(E * 7, 128, DC * 512)),
        "w3t": np.ascontiguousarray(
            np.asarray(L0_ffn_w3, np.float32).transpose(0, 2, 1)
            .reshape(E, DC, 128, 7, 4, 128).transpose(0, 3, 2, 4, 1, 5)
            .reshape(E * 7, 128, DC * 512)),
        "w2t": np.ascontiguousarray(
            np.asarray(L0_ffn_w2, np.float32).transpose(0, 2, 1)
            .reshape(E, NIB, 128, D).transpose(0, 2, 1, 3)
            .reshape(E, 128, NIB * D)),
    }
    nc = build()
    in_maps = []
    for c in range(8):
        m = dict(shared)
        m["xT"] = np.ascontiguousarray(
            x[:, c * NTOK:(c + 1) * NTOK, :].transpose(0, 2, 1)
            .reshape(NSTEP, DC, 128, NTOK).transpose(0, 2, 1, 3)
            .reshape(NSTEP, 128, DC * NTOK))
        in_maps.append(m)
    res = run_bass_kernel_spmd(nc, in_maps, core_ids=list(range(8)))
    parts = [np.asarray(res.results[c]["out"]).reshape(NSTEP, NTOK, D)
             for c in range(8)]
    return np.ascontiguousarray(np.concatenate(parts, axis=1))
